# version 25
# baseline (speedup 1.0000x reference)
import contextlib
import numpy as np
import concourse.bass as bass
import concourse.mybir as mybir
from concourse.bass_utils import run_bass_kernel_spmd

F32 = mybir.dt.float32
BF16 = mybir.dt.bfloat16
I32 = mybir.dt.int32
U8 = mybir.dt.uint8
AF = mybir.ActivationFunctionType
ALU = mybir.AluOpType
AX = mybir.AxisListType.X

NCORES = 8
B = 2
N = 2048
T = B * N
D = 1024
DI = 2816
KD = D // 128
KI = DI // 128
M = N // 128
CM = 12582912.0
PAIRS = ((0, 1024), (1024, 1024), (2048, 768))

ENG = {"sp": "sync", "pe": "tensor", "dv": "vector", "ac": "scalar",
       "pl": "gpsimd"}


class Rec:
    def __init__(self, nc, es):
        self.nc = nc
        self.es = es
        self.block = es.enter_context(nc.Block())
        self.sem = {e: es.enter_context(nc.semaphore("s_" + e)) for e in ENG}
        self.cnt = {e: 0 for e in ENG}
        self.water = {e: {} for e in ENG}
        self.isdma = set()
        self.ops = []

    def o(self, eng, emit, waits=(), inc=1, ctr=None):
        name = ctr or eng
        if name not in self.sem:
            self.sem[name] = self.es.enter_context(
                self.nc.semaphore("s_" + name))
            self.cnt[name] = 0
        ws = tuple(w for w in waits if w is not None)
        self.ops.append((eng, emit, ws, inc, name))
        self.cnt[name] += inc
        if inc >= 16:
            self.isdma.add((name, self.cnt[name]))
        return (name, self.cnt[name])

    def tok(self, eng):
        return (eng, self.cnt[eng])

    def flush(self):
        ops, self.ops = self.ops, []
        for e0 in ("sp", "pe", "dv", "ac", "pl"):
            chunk = [o for o in ops if o[0] == e0]
            if not chunk:
                continue

            def body(q, chunk=chunk, e0=e0):
                wat = self.water[e0]
                for (_e, emit, ws, inc, name) in chunk:
                    for w in ws:
                        if len(w) == 3:
                            we, wv, force = w
                        else:
                            (we, wv), force = w, False
                        if wv <= 0:
                            continue
                        if we == e0 and not force and \
                                (we, wv) not in self.isdma:
                            continue
                        if wv <= wat.get(we, 0) and not force:
                            continue
                        q.wait_ge(self.sem[we], wv)
                        if wv > wat.get(we, 0):
                            wat[we] = wv
                    inst = emit(q)
                    if inc:
                        inst.then_inc(self.sem[name], inc)

            getattr(self.block, ENG[e0])(body)


def F(t):
    return (t[0], t[1], True) if t is not None else None


def build(taps=()):
    nc = bass.Bass("TRN2", target_bir_lowering=False, debug=False)
    es = contextlib.ExitStack()

    x_in = nc.dram_tensor("x", [T, D], F32, kind="ExternalInput").ap()
    ct_in = nc.dram_tensor("c2t", [D, B], F32, kind="ExternalInput").ap()
    w1b1_in = nc.dram_tensor("w1b1", [D, 2], F32, kind="ExternalInput").ap()
    w2_in = nc.dram_tensor("w2", [2 * D, D], F32, kind="ExternalInput").ap()
    wg_in = nc.dram_tensor("wg", [DI, D], F32, kind="ExternalInput").ap()
    wu_in = nc.dram_tensor("wu", [DI, D], F32, kind="ExternalInput").ap()
    wd_in = nc.dram_tensor("wd", [D, DI], F32, kind="ExternalInput").ap()
    wmod_in = nc.dram_tensor("wmod", [D, 6 * D], F32, kind="ExternalInput").ap()
    cst_in = nc.dram_tensor("cst", [128, 8], F32, kind="ExternalInput").ap()
    ici_in = nc.dram_tensor("ici", [128, 2], I32, kind="ExternalInput").ap()
    eye_in = nc.dram_tensor("eye", [128, 128], F32, kind="ExternalInput").ap()
    out_dram = nc.dram_tensor("out", [T, D], F32, kind="ExternalOutput").ap()

    def dscr(name, shape, dt):
        kind = "ExternalOutput" if name in taps else "Internal"
        return nc.dram_tensor(name, shape, dt, kind=kind).ap()

    s_wgn = dscr("s_wgn", [DI, D], BF16)
    s_wun = dscr("s_wun", [DI, D], BF16)
    s_wdn = dscr("s_wdn", [D, DI], BF16)
    s_mod = dscr("s_mod", [B, 6 * D], F32)
    s_norm = [dscr("s_norm%d" % b, [N, D], BF16) for b in range(B)]
    s_xq = [dscr("s_xq%d" % b, [N, D], BF16) for b in range(B)]
    s_hq = dscr("s_hq", [N, DI], BF16)
    s_x1 = dscr("s_x1", [N, D], F32)
    s_flat = dscr("s_flat", [2, N], F32)
    s_cnt = dscr("s_cnt", [B, N], F32)
    s_dbg1 = dscr("s_dbg1", [128, D], F32)
    s_dbg2 = dscr("s_dbg2", [128, D], F32)
    s_dbg3 = dscr("s_dbg3", [128, D], F32)
    s_sv = dscr("s_sv", [B, N], F32)

    g = {}

    def sb(name, shape, dt):
        g[name] = es.enter_context(nc.sbuf_tensor("t_" + name, shape, dt))
        return g[name]

    sb("cst", [128, 8], F32)
    sb("ici", [128, 2], I32)
    sb("eye", [128, 128], F32)
    sb("ct", [128, 2 * KD], F32)
    sb("w1pp", [128, 2 * KD], F32)
    sb("ones", [1, 128], F32)
    for nm in ("sa", "ca", "ga", "cb", "sbv", "gb"):
        sb("r_" + nm, [128, D], F32)
    for i in range(2):
        sb("xr%d" % i, [128, D], F32)
    sb("x1r0", [128, D], F32)
    sb("x1r1", [128, D], F32)
    sb("p1n", [128, D], BF16)
    sb("deadb", [128, D], BF16)
    sb("st1", [128, 40], F32)
    sb("st2", [128, 32], F32)
    sb("svec", [128, M], F32)
    sb("cnt4", [128, 4], F32)
    sb("cntr", [128, M], F32)
    sb("srow", [16, 128], F32)
    sb("srow2", [16, 128], F32)
    sb("sBh", [128, D], BF16)
    sb("cBh", [128, D], F32)
    sb("cctz", [128, D], F32)
    sb("ccta", [128, D], F32)
    sb("mrow", [1, 512], F32)
    sb("q_src", [128, 2 * D], F32)
    sb("q_xn", [128, 2 * D], F32)
    sb("q_t", [128, D], I32)
    sb("q_mask", [128, D], U8)
    sb("q_m1", [128, 2 * D], F32)
    sb("q_qd", [128, 2 * D], BF16)
    sb("q_q", [128, 2 * D], BF16)
    sb("q_bm", [128, 128], F32)
    sb("q_sc", [128, 128], F32)
    sb("q_ri", [128, 128], F32)
    sb("silu_g", [128, D], F32)
    sb("hm", [128, D], F32)

    pool = es.enter_context(nc.sbuf_tensor("pool", [128, 96 * 1024], U8))

    def pview(off_kb, cols, dt):
        nb = mybir.dt.size(dt)
        return pool.ap()[:, off_kb * 1024: off_kb * 1024 + cols * nb].bitcast(dt)

    A1 = pview(0, KD * N, BF16)
    CCT = pview(32, KD * N, BF16)
    W2B = pview(64, 16 * D, BF16)
    XQT = pview(0, KD * N, BF16)
    GUP = [pview(32, 8 * D, BF16), pview(48, 8 * D, BF16)]
    UUP = [pview(64, 8 * D, BF16), pview(80, 8 * D, BF16)]
    WDT = pview(0, KI * D, BF16)
    HQT = pview(48, KI * D, BF16)
    WMS = [pview(0, 6 * D, F32), pview(24, 6 * D, F32)]
    MODEV = pview(60, 512, F32)

    R = Rec(nc, es)
    Tk = {}

    def tk(key):
        return Tk.get(key)

    def emit_repl_row(dram_row_ap, dst_ap, dst_w, ps_rep, evac_eng,
                      extra_src_wait=None, evac_kind="copy", sb_src=None):
        last = None
        nchunks = dst_w // 512
        for h in range(nchunks):
            hp = h % 2
            if sb_src is not None:
                tn, r0 = sb_src
                lt = R.o("sp", lambda q, h=h, tn=tn, r0=r0:
                         q.dma_start(
                             out=g["mrow"][0:1, 0:512],
                             in_=g[tn].ap()[r0 + 4 * h:r0 + 4 * h + 4, :]),
                         (extra_src_wait, tk("mrowf")), 16, ctr="mrowl")
            else:
                lt = R.o("sp", lambda q, h=h, dram_row_ap=dram_row_ap:
                         q.dma_start(
                             out=g["mrow"][0:1, 0:512],
                             in_=dram_row_ap[0:1, h * 512:(h + 1) * 512]),
                         (extra_src_wait, tk("mrowf")), 16, ctr="mrowl")
            mm = R.o("pe", lambda q, hp=hp: q.matmul(
                ps_rep[hp][:], g["ones"][:, 0:128],
                g["mrow"][0:1, 0:512], start=True,
                stop=True), (lt, Tk["ones"], tk("psr%d" % hp)))
            Tk["mrowf"] = mm
            if evac_kind == "plus1":
                ev = R.o(evac_eng, lambda q, h=h, hp=hp, dst_ap=dst_ap:
                         q.tensor_scalar(
                             dst_ap[:, h * 512:(h + 1) * 512], ps_rep[hp][:],
                             1.0, None, ALU.add), (mm,))
            elif evac_eng == "ac":
                ev = R.o("ac", lambda q, h=h, hp=hp, dst_ap=dst_ap:
                         q.activation(dst_ap[:, h * 512:(h + 1) * 512],
                                      ps_rep[hp][:], AF.Copy), (mm,))
            else:
                ev = R.o(evac_eng, lambda q, h=h, hp=hp, dst_ap=dst_ap:
                         q.tensor_copy(dst_ap[:, h * 512:(h + 1) * 512],
                                       ps_rep[hp][:]), (mm,))
            Tk["psr%d" % hp] = ev
            last = ev
            if sb_src is not None:
                Tk[("srowf", sb_src[0])] = lt
        return last

    def emit_gate_fix(dst, pre):
        w0 = (pre, tk("q_free_sel"), tk("q_free_st"),
              tk(("xnfree", 0)), tk(("m1free", 0)))
        gxn = g["q_xn"][:, 0:D]
        gm1 = g["q_m1"][:, 0:D]
        t1 = R.o("dv", lambda q, dst=dst: q.tensor_scalar(
            g["q_t"][:], dst.ap().bitcast(I32), g["ici"][:, 0:1], None,
            ALU.bitwise_and), w0)
        t1 = R.o("dv", lambda q: q.tensor_scalar(
            gm1, g["q_t"].ap().bitcast(F32), 0.25, None, ALU.max), ())
        t2 = R.o("dv", lambda q: q.tensor_scalar(
            gxn, gm1, -1.0, None, ALU.mult), ())
        t3 = R.o("dv", lambda q, dst=dst: q.tensor_scalar(
            g["q_mask"][:], dst[:], 0.0, None, ALU.is_lt), ())
        last = R.o("dv", lambda q, dst=dst: q.select(
            dst[:], g["q_mask"][:], gxn, gm1), ())
        Tk["q_free_gate"] = last
        Tk[("xnfree", 0)] = last
        Tk[("m1free", 0)] = last
        return last

    def emit_repl_mod(b, v, dst_name, kind, ps_rep):
        dst = g[dst_name]
        last = emit_repl_row(s_mod[b:b + 1, v * D:(v + 1) * D], dst.ap(), D,
                             ps_rep, "dv", extra_src_wait=Tk["mod_done"],
                             evac_kind=("plus1" if kind == "plus1"
                                        else "copy"))
        if kind == "gate":
            last = emit_gate_fix(dst, last)
            if b == 1 and dst_name == "r_ga" and "s_dbg2" in taps:
                R.o("ac", lambda q, dst=dst: q.dma_start(
                    out=s_dbg2, in_=dst[:]), (last,), 16, ctr="dbg")
            if b == 1 and dst_name == "r_gb" and "s_dbg3" in taps:
                R.o("ac", lambda q, dst=dst: q.dma_start(
                    out=s_dbg3, in_=dst[:]), (last,), 16, ctr="dbg")
        Tk[dst_name] = last

    qctr = [0]

    def emit_quant(src_ap, W, store_emit, pre_waits=(), store_waits=()):
        i = qctr[0] = qctr[0] + 1
        nb = W // 16
        par = (i % 2) * 64
        pw = (i % 2) * D
        bm = g["q_bm"][:, par:par + nb]
        sc = g["q_sc"][:, par:par + nb]
        ri = g["q_ri"][:, par:par + nb]
        xn = g["q_xn"][:, pw:pw + W]
        tq = g["q_t"][:, 0:W]
        msk = g["q_mask"][:, 0:W]
        m1 = g["q_m1"][:, pw:pw + W]
        qq = g["q_q"][:, pw:pw + W]
        qd = g["q_qd"][:, pw:pw + W]
        p2 = i % 2

        w0 = tuple(pre_waits) + (tk("q_free_gate"), Tk["consts"])
        t = R.o("dv", lambda q: q.reduce_max(
            bm, src_ap.rearrange("p (b s) -> p b s", s=16), axis=AX,
            apply_absolute_value=True), w0)
        t = R.o("dv", lambda q: q.tensor_scalar(
            sc, bm, g["cst"][:, 0:1], g["cst"][:, 1:2], ALU.mult, ALU.max),
            (F(t),))
        tri = R.o("dv", lambda q: q.reciprocal(ri, sc), (F(t),))
        txn = R.o("pl", lambda q: q.tensor_tensor(
            xn.rearrange("p (b s) -> p b s", s=16),
            src_ap.rearrange("p (b s) -> p b s", s=16),
            ri.unsqueeze(2).broadcast_to([128, nb, 16]), ALU.mult),
            (F(tri), tk(("xnfree", p2))) + tuple(pre_waits))
        Tk["q_xn_done"] = txn
        tam = R.o("dv", lambda q: q.tensor_scalar(
            tq, xn.bitcast(I32), g["ici"][:, 0:1], None, ALU.bitwise_and),
            (txn,))
        tmask = R.o("dv", lambda q: q.tensor_scalar(
            msk, tq, 0x3F600000, None, ALU.is_gt), ())
        t = R.o("dv", lambda q: q.tensor_scalar(
            tq, xn.bitcast(I32), 21, None, ALU.logical_shift_right), ())
        t = R.o("dv", lambda q: q.tensor_scalar(tq, tq, 1, None, ALU.add),
                ())
        thi = R.o("dv", lambda q: q.tensor_scalar(
            tq, tq, 1, 22, ALU.logical_shift_right, ALU.logical_shift_left),
            ())
        tm = R.o("ac", lambda q: q.activation(m1, xn, AF.Copy, bias=CM,
                                              scale=2.0),
                 (txn, tk(("m1free", p2))))
        tm = R.o("ac", lambda q: q.activation(m1, m1, AF.Copy, bias=-CM / 2,
                                              scale=0.5), (F(tm),))
        Tk[("xnfree", p2)] = tm
        tsel = R.o("dv", lambda q: q.select(qq, msk, tq.bitcast(F32), m1),
                   (tm, tk(("qfree", p2))))
        Tk["q_free_sel"] = tsel
        Tk[("m1free", p2)] = tsel
        tqd = R.o("pl", lambda q: q.tensor_tensor(
            qd.rearrange("p (b s) -> p b s", s=16),
            qq.rearrange("p (b s) -> p b s", s=16),
            sc.unsqueeze(2).broadcast_to([128, nb, 16]), ALU.mult),
            (tsel, tk(("qdfree", p2))))
        Tk[("qfree", p2)] = tqd
        Tk["q_qd_done"] = tqd
        tst = R.o("ac", lambda q: store_emit(q, qd),
                  (tqd,) + tuple(store_waits), 16, ctr="qst")
        Tk[("qdfree", p2)] = tst
        Tk["q_free_st"] = tst
        return tst

    def ffn_quant_jobs():
        jobs = []
        for src, dst, nm in ((wg_in, s_wgn, "wg"), (wu_in, s_wun, "wu")):
            for c in range(KI):
                jobs.append((src, dst, nm, c, 0, D))
        for c in range(KD):
            for (off, W) in PAIRS:
                jobs.append((wd_in, s_wdn, "wd", c, off, W))
        for (j, (src, dst, nm, c, off, W)) in enumerate(jobs):
            par = j % 2
            qs = g["q_src"][:, par * D:par * D + W]
            lt = R.o("sp", lambda q, src=src, c=c, off=off, W=W, qs=qs:
                     q.dma_start(out=qs,
                                 in_=src[c * 128:(c + 1) * 128,
                                         off:off + W]),
                     (tk(("q_src_free", par)),), 16, ctr="wql")

            def store_emit(q, qd, dst=dst, c=c, off=off, W=W):
                return q.dma_start(
                    out=dst[c * 128:(c + 1) * 128, off:off + W], in_=qd)
            tst = emit_quant(qs, W, store_emit, pre_waits=(lt,))
            Tk[("q_src_free", par)] = Tk["q_xn_done"]
            Tk["wq_%s_%d" % (nm, c)] = tst
            yield

    ffnq = ffn_quant_jobs()
    ffnq_alive = [True]

    def ffnq_step(k=1):
        if not ffnq_alive[0]:
            return
        for _ in range(k):
            try:
                next(ffnq)
            except StopIteration:
                ffnq_alive[0] = False
                return

    def p1_chunk(b, m):
        slot = m % 2
        xr = g["xr%d" % slot]
        par = m % 2
        st = g["st1"]
        base = par * 20
        bno = st[:, base:base + 12]
        mean = st[:, base + 12:base + 13]
        var = st[:, base + 13:base + 14]
        rstd = st[:, base + 14:base + 15]
        nmr = st[:, base + 15:base + 16]
        sq2 = st[:, base + 16:base + 17]
        rn = st[:, base + 17:base + 18]
        row0 = b * N + m * 128

        if (b, m) in ((0, 0), (0, 1)):
            tl = Tk[("p1xl", m)]
        else:
            tl = R.o("sp", lambda q, xr=xr, row0=row0: q.dma_start(
                out=xr[:], in_=x_in[row0:row0 + 128, :]),
                (tk(("xrf", slot)),), 16, ctr="xr%d" % slot)
        pf = tk(("p1fin", par))
        t = R.o("dv", lambda q, xr=xr, bno=bno: q.bn_stats(
            bno[:, 0:6], xr[:, 0:512]), (tl, pf))
        tb = R.o("dv", lambda q, xr=xr, bno=bno: q.bn_stats(
            bno[:, 6:12], xr[:, 512:1024]), ())
        tg = R.o("dv", lambda q, bno=bno: q.bn_aggr(
            st[:, base + 12:base + 14], bno), (F(tb),))
        t = R.o("ac", lambda q, var=var, rstd=rstd: q.activation(
            rstd, var, AF.Ln, bias=g["cst"][:, 2:3]), (F(tg),))
        t = R.o("ac", lambda q, rstd=rstd: q.activation(
            rstd, rstd, AF.Exp, scale=-0.5), (F(t),))
        tnm = R.o("dv", lambda q, mean=mean, rstd=rstd, nmr=nmr:
                  q.tensor_scalar(nmr, mean, rstd, -1.0, ALU.mult,
                                  ALU.mult), (F(t),))
        tt1 = R.o("ac", lambda q, xr=xr, nmr=nmr, rstd=rstd: q.activation(
            xr[:], xr[:], AF.Identity, bias=nmr, scale=rstd),
            (F(tnm),))
        th = R.o("dv", lambda q, xr=xr: q.tensor_mul(
            xr[:], xr[:], g["r_ca"][:]), (F(tt1), Tk["r_ca"]))
        th = R.o("pl", lambda q, xr=xr: q.tensor_add(
            xr[:], xr[:], g["r_sa"][:]), (th, Tk["r_sa"]))
        ta2 = R.o("ac", lambda q, xr=xr, sq2=sq2: q.activation(
            g["deadb"][:], xr[:], AF.Square, accum_out=sq2), (th,))
        t = R.o("ac", lambda q, sq2=sq2, rn=rn: q.activation(
            rn, sq2, AF.Ln, bias=g["cst"][:, 3:4]), (F(ta2),))
        t = R.o("ac", lambda q, rn=rn, m=m: q.activation(
            g["svec"][:, m:m + 1], rn, AF.Exp, scale=0.5),
            (F(t), tk("svec_free")))
        t = R.o("ac", lambda q, rn=rn: q.activation(
            rn, rn, AF.Exp, scale=-0.5), (F(t),))
        tn = R.o("dv", lambda q, xr=xr, rn=rn: q.tensor_scalar(
            g["p1n"][:], xr[:], rn, None, ALU.mult),
            (F(t), tk("p1n_free")))
        Tk[("xrf", slot)] = tn
        ts = R.o("ac", lambda q, b=b, m=m: q.dma_start(
            out=s_norm[b][m * 128:(m + 1) * 128, :], in_=g["p1n"][:]),
            (tn,), 16, ctr="nrmst")
        Tk["p1n_free"] = ts
        Tk[("p1fin", par)] = tn
        if m == M - 1:
            Tk[("p1done", b)] = ts

    def emit_col_to_dram(src_col_ap, dram_row, ps_t16, srctok, tn):
        t = R.o("pe", lambda q, src_col_ap=src_col_ap: q.transpose(
            ps_t16[0:M, :], src_col_ap, g["eye"][:]),
            (srctok, Tk["consts"], tk("pst16")))
        t = R.o("dv", lambda q, tn=tn: q.tensor_copy(
            g[tn][0:M, :], ps_t16[0:M, :]),
            (t, tk(("srowf", tn)), tk(("srowfl", tn))))
        Tk["pst16"] = t
        tsv = t
        t = R.o("pl", lambda q, dram_row=dram_row, tn=tn: q.dma_start(
            out=dram_row.rearrange("a (p f) -> (a p) f", p=M),
            in_=g[tn].ap()[0:M, :]),
            (t,), 16, ctr="fl")
        Tk[("srowfl", tn)] = t
        return t, tsv

    t = R.o("sp", lambda q: q.dma_start(out=g["cst"][:], in_=cst_in), (), 16,
            ctr="cstl")
    t = R.o("sp", lambda q: q.dma_start(out=g["ici"][:], in_=ici_in), (), 16,
            ctr="cstl")
    t = R.o("sp", lambda q: q.dma_start(out=g["eye"][:], in_=eye_in), (), 16,
            ctr="cstl")
    t = R.o("sp", lambda q: q.dma_start(
        out=g["ct"].ap().rearrange("p (k c) -> p k c", c=B),
        in_=ct_in.rearrange("(k p) c -> p k c", p=128)), (), 16, ctr="cstl")
    t = R.o("sp", lambda q: q.dma_start(
        out=g["w1pp"].ap().rearrange("p (k c) -> p k c", c=2),
        in_=w1b1_in.rearrange("(k p) c -> p k c", p=128)), (), 16, ctr="cstl")
    Tk["consts"] = t
    Tk["ones"] = R.o("dv", lambda q: q.memset(g["ones"][:], 1.0), ())
    Tk["ctsilu"] = R.o("ac", lambda q: q.activation(
        g["ct"][:], g["ct"][:], AF.Silu), (Tk["consts"],))

    with nc.psum_tensor("psma0", [128, 512], F32) as pa0, \
         nc.psum_tensor("psma1", [128, 512], F32) as pa1, \
         nc.psum_tensor("psma2", [128, 512], F32) as pa2, \
         nc.psum_tensor("psma3", [128, 512], F32) as pa3, \
         nc.psum_tensor("psr0a", [128, 512], F32) as pr0, \
         nc.psum_tensor("psr1a", [128, 512], F32) as pr1:
        pma = [pa0, pa1, pa2, pa3]
        ps_rep = [pr0, pr1]
        slab_tok = [None, None]
        for pm in range(2):
            Tk[("p1xl", pm)] = R.o(
                "sp", lambda q, pm=pm: q.dma_start(
                    out=g["xr%d" % pm][:],
                    in_=x_in[pm * 128:(pm + 1) * 128, :]), (), 16,
                ctr="xr%d" % pm)
        mm = None
        for k in range(KD):
            sl = k % 2
            lt = R.o("sp", lambda q, k=k, sl=sl: q.dma_start(
                out=WMS[sl][:], in_=wmod_in[k * 128:(k + 1) * 128, :]),
                (slab_tok[sl],), 16, ctr="wm%d" % sl)
            for n in range(12):
                po = 32 * (n % 3)
                mm = R.o("pe", lambda q, k=k, n=n, sl=sl, po=po: q.matmul(
                    pma[n // 3][po:po + 2, :],
                    g["ct"][:, 2 * k:2 * k + 2],
                    WMS[sl][:, n * 512:(n + 1) * 512],
                    start=(k == 0), stop=(k == KD - 1)),
                    (lt, Tk["ctsilu"]))
            slab_tok[sl] = mm
            ffnq_step(1)
        Tk["mod_mm_done"] = mm
        tmst = None
        for bk in range(4):
            tev = R.o("dv", lambda q, bk=bk: q.tensor_copy(
                MODEV[0:66, :], pma[bk][0:66, :]), (mm, tmst))
            for c in range(B):
                tmst = R.o(
                    "ac", lambda q, bk=bk, c=c: q.dma_start(
                        out=s_mod[c:c + 1, 3 * bk * 512:(3 * bk + 3) * 512]
                        .rearrange("a (n w) -> (a n) w", w=512),
                        in_=MODEV[c:c + 65:32, :]),
                    (tev,), 16, ctr="modst")
        Tk["mod_done"] = tmst

        emit_repl_mod(0, 0, "r_sa", "plain", ps_rep)
        emit_repl_mod(0, 1, "r_ca", "plus1", ps_rep)
        emit_repl_mod(0, 2, "r_ga", "gate", ps_rep)
        emit_repl_mod(0, 3, "r_sbv", "plain", ps_rep)
        emit_repl_mod(0, 4, "r_cb", "plus1", ps_rep)
        emit_repl_mod(0, 5, "r_gb", "gate", ps_rep)

        Tk["w2pre"] = R.o("pl", lambda q: q.dma_start(
            out=W2B.rearrange("p (k d) -> p k d", d=D),
            in_=w2_in.rearrange("(k p) d -> p k d", p=128)),
            (Tk["mod_mm_done"],), 16, ctr="w2l")
        for m in range(M):
            p1_chunk(0, m)
            ffnq_step(2)
        R.flush()

    for b in range(B):
        with nc.psum_tensor("ps_sim%d" % b, [128, 4 * 512], F32) as ps_sim, \
             nc.psum_tensor("pst16%d" % b, [16, 128], F32) as ps_t16, \
             nc.psum_tensor("psr0b%d" % b, [128, 512], F32) as pr0, \
             nc.psum_tensor("psr1b%d" % b, [128, 512], F32) as pr1:
            ps_rep = [pr0, pr1]
            for k in range(KD):
                Tk[("A1", k)] = R.o("sp", lambda q, k=k, b=b:
                          q.dma_start_transpose(
                    A1[:, k * N:(k + 1) * N],
                    s_norm[b][:, k * 128:(k + 1) * 128]),
                    (tk(("p1done", b)), tk("pool_lo_free"),
                     tk("pool_hi_free"), Tk["mod_mm_done"]), 16, ctr="a1l")

            if b == 1:
                emit_repl_mod(1, 2, "r_ga", "gate", ps_rep)
                emit_repl_mod(1, 3, "r_sbv", "plain", ps_rep)
                emit_repl_mod(1, 4, "r_cb", "plus1", ps_rep)
                emit_repl_mod(1, 5, "r_gb", "gate", ps_rep)

            tfl, tsv = emit_col_to_dram(g["svec"][:, 0:M], s_flat[0:1, :],
                                        ps_t16, tk(("p1done", b)),
                                        "srow2")
            Tk["svec_free"] = tsv
            if "s_sv" in taps:
                R.o("pl", lambda q, b=b: q.dma_start(
                    out=s_sv[b:b + 1, :], in_=s_flat[0:1, :]), (tfl,), 16, ctr="fl")

            mmlast = None
            for n in range(4):
                Tk[("simev", -1, n)] = tk(("simev", M - 1, n))
            for m in range(M):
                for k in range(KD):
                    for n in range(4):
                        w = [Tk[("A1", k)]]
                        if k == 0:
                            w.append(tk(("simev", m - 1, n)))
                        mmlast = R.o("pe", lambda q, k=k, n=n, m=m: q.matmul(
                            ps_sim[:, n * 512:(n + 1) * 512],
                            A1[:, k * N + m * 128:k * N + m * 128 + 128],
                            A1[:, k * N + n * 512:k * N + (n + 1) * 512],
                            start=(k == 0), stop=(k == KD - 1)), w)
                Tk[("simmm", m)] = mmlast
                for n in (0, 1, 2, 3):
                    Tk[("simev", m, n)] = R.o(
                        "ac", lambda q, n=n: q.activation(
                            g["deadb"][:, 512:1024],
                            ps_sim[:, n * 512:(n + 1) * 512], AF.Relu,
                            accum_out=g["cnt4"][:, n:n + 1]),
                        (Tk[("simmm", m)], tk(("cnt", m - 1))))
                Tk[("cnt", m)] = R.o("dv", lambda q, m=m: q.reduce_sum(
                    g["cntr"][:, m:m + 1], g["cnt4"][:, 0:4], axis=AX),
                    (Tk[("simev", m, 3)],))
                if b == 0:
                    ffnq_step(2)

            tfc, tcv = emit_col_to_dram(g["cntr"][:, 0:M], s_flat[1:2, :],
                                        ps_t16, F(Tk[("cnt", M - 1)]),
                                        "srow")
            if "s_cnt" in taps:
                R.o("pl", lambda q, b=b: q.dma_start(
                    out=s_cnt[b:b + 1, :], in_=s_flat[1:2, :]), (tfc,), 16, ctr="fl")

            for half in range(2):
                hs = half * 1024
                tcb = emit_repl_row(None, g["cBh"].ap(),
                                    1024, ps_rep, "dv", extra_src_wait=tcv,
                                    sb_src=("srow", 8 * half))
                tsb = emit_repl_row(None, g["sBh"].ap(),
                                    1024, ps_rep, "ac", extra_src_wait=tsv,
                                    sb_src=("srow2", 8 * half))
                for k in range(KD):
                    tz = R.o("dv", lambda q, k=k: q.tensor_scalar(
                        g["cctz"][:], g["cBh"][:],
                        g["w1pp"][:, 2 * k:2 * k + 1],
                        g["w1pp"][:, 2 * k + 1:2 * k + 2],
                        ALU.mult, ALU.add), (tcb, tk("cct_free")))
                    t = R.o("dv", lambda q: q.tensor_scalar(
                        g["ccta"].ap().bitcast(I32),
                        g["cctz"].ap().bitcast(I32),
                        g["ici"][:, 0:1], None, ALU.bitwise_and),
                        (F(tz), tk("ccta_free")))
                    t = R.o("ac", lambda q: q.activation(
                        g["ccta"][:], g["ccta"][:], AF.Exp, scale=-1.0),
                        (t,))
                    t = R.o("ac", lambda q: q.activation(
                        g["ccta"][:], g["ccta"][:], AF.Ln, bias=1.0),
                        (F(t),))
                    t = R.o("dv", lambda q, k=k, hs=hs:
                            q.scalar_tensor_tensor(
                        CCT[:, k * N + hs:k * N + hs + 1024], g["cctz"][:],
                        0.0, g["ccta"][:], ALU.max, ALU.add),
                        (t, tk("pool_lo_free"), Tk[("simmm", M - 1)]))
                    Tk["cct_free"] = t
                    Tk["ccta_free"] = t
                    Tk[("CCT", k, half)] = t
                for k in range(KD):
                    Tk[("hT", k, half)] = R.o(
                        "dv", lambda q, k=k, hs=hs: q.tensor_mul(
                            A1[:, k * N + hs:k * N + hs + 1024],
                            A1[:, k * N + hs:k * N + hs + 1024],
                            g["sBh"][:]), (tsb, Tk[("simmm", M - 1)]))
            R.flush()

        with nc.psum_tensor("ps_at0%d" % b, [128, D], F32) as pa0, \
             nc.psum_tensor("ps_at1%d" % b, [128, D], F32) as pa1, \
             nc.psum_tensor("ps_at2%d" % b, [128, D], F32) as pa2, \
             nc.psum_tensor("psr0c%d" % b, [128, 512], F32) as pr0, \
             nc.psum_tensor("psr1c%d" % b, [128, 512], F32) as pr1:
            ps_at = [pa0, pa1, pa2]
            ps_rep = [pr0, pr1]
            tw2 = Tk["w2pre"]

            if b == 0:
                emit_repl_mod(1, 0, "r_sa", "plain", ps_rep)
                emit_repl_mod(1, 1, "r_ca", "plus1", ps_rep)

            mmlast = None
            Tk[("hmfw2", 0)] = tk("hm_free")
            Tk[("hmfw2", 1)] = tk(("x1rf", 0))
            for dd in range(1, 4):
                Tk[("atev", -dd)] = tk(("atev", M - dd))
            for m in range(M):
                slot = m % 2
                xr = g["xr%d" % slot]
                row0 = b * N + m * 128
                tl = R.o("sp", lambda q, xr=xr, row0=row0: q.dma_start(
                    out=xr[:], in_=x_in[row0:row0 + 128, :]),
                    (tk(("xrf", slot)), tk(("x1stp", slot))), 16,
                    ctr="xr%d" % slot)
                half = m // 8
                for k in range(16):
                    kk = k % 8
                    lhs = A1 if k < 8 else CCT
                    w = [Tk[("hT", kk, half)] if k < 8
                         else Tk[("CCT", kk, half)], tw2]
                    if k == 0:
                        w.append(tk(("atev", m - 3)))
                    for n in range(2):
                        mmlast = R.o("pe", lambda q, k=k, kk=kk, n=n, m=m,
                                     lhs=lhs: q.matmul(
                            ps_at[m % 3][:, n * 512:(n + 1) * 512],
                            lhs[:, kk * N + m * 128:kk * N + m * 128 + 128],
                            W2B[:, k * D + n * 512:k * D + (n + 1) * 512],
                            start=(k == 0), stop=(k == 15)), w)
                Tk[("atmm", m)] = mmlast
                x1t = xr
                t1 = R.o("dv", lambda q, m=m: q.tensor_mul(
                    g["silu_g"][:], ps_at[m % 3][:], g["r_ga"][:]),
                    (Tk[("atmm", m)], Tk["r_ga"], tk("t1_free")))
                Tk[("atev", m)] = t1
                tx1 = R.o("pl", lambda q, x1t=x1t: q.tensor_add(
                    x1t[:], g["silu_g"][:], x1t[:]), (t1, tl))
                Tk["t1_free"] = tx1
                tsx = R.o("ac", lambda q, x1t=x1t, m=m: q.dma_start(
                    out=s_x1[m * 128:(m + 1) * 128, :], in_=x1t[:]),
                    (tx1, tk("s_x1_free")), 16, ctr="x1st%d" % (m % 2))
                Tk[("s_x1", m)] = tsx
                Tk[("x1stp", m % 2)] = tsx

                par = m % 2
                st = g["st2"]
                base = par * 16
                bno = st[:, base:base + 12]
                mean = st[:, base + 12:base + 13]
                var = st[:, base + 13:base + 14]
                rstd = st[:, base + 14:base + 15]
                nmr = st[:, base + 15:base + 16]
                lf = tk(("ln2fin", par))
                t = R.o("dv", lambda q, x1t=x1t, bno=bno: q.bn_stats(
                    bno[:, 0:6], x1t[:, 0:512]), (tx1, lf))
                tb = R.o("dv", lambda q, x1t=x1t, bno=bno: q.bn_stats(
                    bno[:, 6:12], x1t[:, 512:1024]), ())
                tg = R.o("dv", lambda q, base=base: q.bn_aggr(
                    st[:, base + 12:base + 14], st[:, base:base + 12]),
                    (F(tb),))
                t = R.o("ac", lambda q, var=var, rstd=rstd: q.activation(
                    rstd, var, AF.Ln, bias=g["cst"][:, 2:3]), (F(tg),))
                t = R.o("ac", lambda q, rstd=rstd: q.activation(
                    rstd, rstd, AF.Exp, scale=-0.5), (F(t),))
                t = R.o("dv", lambda q, mean=mean, rstd=rstd, nmr=nmr:
                        q.tensor_scalar(nmr, mean, rstd, -1.0, ALU.mult,
                                        ALU.mult), (F(t),))
                hmw = g["hm"] if par == 0 else g["x1r0"]
                th2 = R.o("ac", lambda q, x1t=x1t, nmr=nmr, rstd=rstd,
                          hmw=hmw: q.activation(hmw[:], x1t[:], AF.Identity,
                                                bias=nmr, scale=rstd),
                          (F(t), tk(("hmfw2", par))))
                Tk[("ln2fin", par)] = th2
                Tk[("xrf", slot)] = th2
                th2 = R.o("pl", lambda q, hmw=hmw: q.tensor_mul(
                    hmw[:], hmw[:], g["r_cb"][:]),
                    (th2, Tk["r_cb"]))
                th2 = R.o("pl", lambda q, hmw=hmw: q.tensor_add(
                    hmw[:], hmw[:], g["r_sbv"][:]),
                    (th2, Tk["r_sbv"]))

                def store_xq(q, qd, b=b, m=m):
                    return q.dma_start(
                        out=s_xq[b][m * 128:(m + 1) * 128, :], in_=qd)
                tst = emit_quant(hmw[:], D, store_xq, pre_waits=(th2,))
                Tk[("hmfw2", par)] = Tk["q_xn_done"]
                ffnq_step(1)
            Tk["A1_lastuse"] = Tk[("atmm", M - 1)]
            Tk["xq_done"] = Tk["q_free_st"]
            Tk["hm_free"] = Tk[("hmfw2", 0)]
            R.flush()

        with nc.psum_tensor("ps_g0%d" % b, [128, D], F32) as pg0, \
             nc.psum_tensor("ps_g1%d" % b, [128, D], F32) as pg1, \
             nc.psum_tensor("ps_u0%d" % b, [128, D], F32) as pu0, \
             nc.psum_tensor("ps_u1%d" % b, [128, D], F32) as pu1:
            ps_g = [pg0, pg1]
            ps_u = [pu0, pu1]
            for k in range(KD):
                Tk[("XQT", k)] = R.o("sp", lambda q, k=k, b=b:
                          q.dma_start_transpose(
                    XQT[:, k * N:(k + 1) * N],
                    s_xq[b][:, k * 128:(k + 1) * 128]),
                    (Tk["xq_done"], Tk["A1_lastuse"]), 16, ctr="xqtl")

            p1_left = list(range(M)) if b == 0 else []
            it = 0
            Tk[("hmfree", 0)] = tk("hm_free")
            Tk[("guev", -2)] = tk(("guev", 3 * M - 2))
            Tk[("guev", -1)] = tk(("guev", 3 * M - 1))
            for (p, (poff, Wp)) in enumerate(PAIRS):
                pp = p % 2
                wq_wait = Tk["wq_wu_%d" % (KI - 1)]
                for k in range(KD):
                    tgl = R.o(
                        "sp", lambda q, k=k, pp=pp, poff=poff, Wp=Wp:
                        q.dma_start_transpose(
                            GUP[pp][:, k * D:k * D + Wp],
                            s_wgn[poff:poff + Wp, k * 128:(k + 1) * 128]),
                        (wq_wait, tk(("gup_free", pp)), Tk["A1_lastuse"],
                         tk("pool_hi_free")), 16, ctr="gupl%d" % pp)
                    tul = R.o(
                        "sp", lambda q, k=k, pp=pp, poff=poff, Wp=Wp:
                        q.dma_start_transpose(
                            UUP[pp][:, k * D:k * D + Wp],
                            s_wun[poff:poff + Wp, k * 128:(k + 1) * 128]),
                        (tk(("uup_free", pp)), Tk["A1_lastuse"],
                         tk("pool_hi_free")), 16, ctr="uupl%d" % pp)
                    Tk[("gup", pp, k)] = tgl
                    Tk[("uup", pp, k)] = tul
                nh = (Wp + 511) // 512
                for m in range(M):
                    itp = it % 2
                    mmlast = None
                    for k in range(KD):
                        w = [Tk[("XQT", k)], Tk[("gup", pp, k)],
                             Tk[("uup", pp, k)]]
                        if k == 0:
                            w.append(tk(("guev", it - 2)))
                        for n in range(nh):
                            c0 = n * 512
                            c1 = min(Wp, (n + 1) * 512)
                            R.o("pe", lambda q, k=k, m=m, pp=pp, c0=c0,
                                c1=c1, itp=itp: q.matmul(
                                ps_g[itp][:, c0:c1],
                                XQT[:, k * N + m * 128:k * N + m * 128 + 128],
                                GUP[pp][:, k * D + c0:k * D + c1],
                                start=(k == 0), stop=(k == KD - 1)), w)
                            mmlast = R.o("pe", lambda q, k=k, m=m, pp=pp,
                                         c0=c0, c1=c1, itp=itp: q.matmul(
                                ps_u[itp][:, c0:c1],
                                XQT[:, k * N + m * 128:k * N + m * 128 + 128],
                                UUP[pp][:, k * D + c0:k * D + c1],
                                start=(k == 0), stop=(k == KD - 1)), ())
                    Tk[("gumm", it)] = mmlast
                    hmv = (g["hm"] if itp == 0 else g["q_src"])[:, 0:Wp]
                    hw = [None, tk(("hmfree", itp))]
                    if it == 1:
                        hw += [tk(("q_src_free", 0)), tk(("q_src_free", 1))]
                    tsg = R.o("ac", lambda q, Wp=Wp, itp=itp: q.activation(
                        g["silu_g"][:, 0:Wp], ps_g[itp][:, 0:Wp], AF.Silu),
                        (Tk[("gumm", it)], tk("t1_free")))
                    hw[0] = tsg
                    thm = R.o("dv", lambda q, Wp=Wp, itp=itp, hmv=hmv:
                              q.tensor_mul(hmv, g["silu_g"][:, 0:Wp],
                                           ps_u[itp][:, 0:Wp]), tuple(hw))
                    Tk["t1_free"] = thm
                    Tk[("guev", it)] = thm

                    def store_hq(q, qd, m=m, poff=poff, Wp=Wp):
                        return q.dma_start(
                            out=s_hq[m * 128:(m + 1) * 128, poff:poff + Wp],
                            in_=qd)
                    tst = emit_quant(hmv, Wp, store_hq,
                                     pre_waits=(thm,),
                                     store_waits=(tk("s_hq_free"),))
                    Tk[("hmfree", itp)] = Tk["q_xn_done"]
                    Tk[("hq", p, m)] = tst
                    it += 1
                Tk[("gup_free", pp)] = Tk[("gumm", it - 1)]
                Tk[("uup_free", pp)] = Tk[("gumm", it - 1)]
            Tk["hq_done"] = Tk[("hq", 2, M - 1)]
            Tk["gu_lastmm"] = Tk[("gumm", it - 1)]
            Tk["hm_free"] = Tk[("hmfree", 0)]
            R.flush()

        with nc.psum_tensor("ps_d0%d" % b, [128, D], F32) as pd0, \
             nc.psum_tensor("ps_d1%d" % b, [128, D], F32) as pd1, \
             nc.psum_tensor("ps_d2%d" % b, [128, D], F32) as pd2, \
             nc.psum_tensor("ps_d3%d" % b, [128, D], F32) as pd3:
            ps_d = [pd0, pd1, pd2, pd3]
            for ki in range(KI):
                twd = R.o("sp", lambda q, ki=ki:
                          q.dma_start_transpose(
                    WDT[:, ki * D:(ki + 1) * D],
                    s_wdn[:, ki * 128:(ki + 1) * 128]),
                    (Tk["wq_wd_%d" % (KD - 1)], Tk["gu_lastmm"],
                     Tk["xq_done"]), 16, ctr="wdtl")
                Tk[("WDT", ki)] = twd
            gbtok = Tk["r_gb"]
            for dd in range(1, 5):
                Tk[("dnev", -dd)] = tk(("dnev", M - dd))

            mmlast = None
            for half in range(2):
                for ki in range(KI):
                    thq = R.o("sp", lambda q, ki=ki, half=half:
                              q.dma_start_transpose(
                        HQT[:, ki * D:(ki + 1) * D],
                        s_hq[half * 1024:(half + 1) * 1024,
                             ki * 128:(ki + 1) * 128]),
                        (Tk["hq_done"], Tk["gu_lastmm"],
                         tk(("hqt_used", ki))), 16, ctr="hqtl")
                    Tk[("HQT", ki)] = thq
                for mm_i in range(8):
                    m = half * 8 + mm_i
                    x1r = g["x1r0" if m % 2 == 0 else "x1r1"]
                    tl = R.o("sp", lambda q, x1r=x1r, m=m: q.dma_start(
                        out=x1r[:], in_=s_x1[m * 128:(m + 1) * 128, :]),
                        (Tk[("s_x1", m)], tk(("x1rf", m % 2)))
                        + ((tk("xq_done"),) if m == 0 else ()), 16,
                        ctr="x1r%d" % (m % 2))
                    for ki in range(KI):
                        w = [Tk[("WDT", ki)], Tk[("HQT", ki)]]
                        if ki == 0:
                            w.append(tk(("dnev", m - 4)))
                        for n in range(2):
                            mmlast = R.o("pe", lambda q, ki=ki, m=m,
                                         mm_i=mm_i, n=n: q.matmul(
                                ps_d[m % 4][:, n * 512:(n + 1) * 512],
                                HQT[:, ki * D + mm_i * 128:
                                    ki * D + mm_i * 128 + 128],
                                WDT[:, ki * D + n * 512:
                                    ki * D + (n + 1) * 512],
                                start=(ki == 0), stop=(ki == KI - 1)), w)
                        Tk[("hqt_used", ki)] = mmlast
                    Tk[("dnmm", m)] = mmlast
                    t3 = R.o("dv", lambda q, m=m: q.tensor_mul(
                        g["silu_g"][:], ps_d[m % 4][:], g["r_gb"][:]),
                        (Tk[("dnmm", m)], gbtok, tk("t1_free")))
                    Tk[("dnev", m)] = t3
                    to = R.o("pl", lambda q, x1r=x1r:
                             q.tensor_add(x1r[:], g["silu_g"][:], x1r[:]),
                             (t3, tl))
                    Tk["t1_free"] = to
                    tso = R.o("ac", lambda q, x1r=x1r, m=m, b=b:
                              q.dma_start(
                        out=out_dram[b * N + m * 128:
                                     b * N + (m + 1) * 128, :],
                        in_=x1r[:]), (to,), 16, ctr="outst%d" % (m % 2))
                    Tk[("x1rf", m % 2)] = tso
                    Tk[("outst", b, m)] = tso
                    if b == 0:
                        p1_chunk(1, m)
            Tk["s_x1_free"] = tk(("x1rf", 1))
            Tk["s_hq_free"] = Tk[("dnmm", M - 1)]
            Tk["pool_lo_free"] = Tk[("dnmm", M - 1)]
            Tk["pool_hi_free"] = Tk[("dnmm", M - 1)]
            if b == 0:
                Tk["w2pre"] = R.o("pl", lambda q: q.dma_start(
                    out=W2B.rearrange("p (k d) -> p k d", d=D),
                    in_=w2_in.rearrange("(k p) d -> p k d", p=128)),
                    (Tk["pool_hi_free"],), 16, ctr="w2l")
            R.flush()

    v0 = R.cnt["outst0"]
    v1 = R.cnt["outst1"]
    R.o("sp", lambda q: q.wait_ge(R.sem["outst0"], v0), (), 0)
    R.o("sp", lambda q: q.wait_ge(R.sem["outst1"], v1), (), 0)
    R.flush()

    es.close()
    return nc


def _in_maps(L0_attn_data, L0_attn_W1, L0_attn_b1, L0_attn_W2, L0_ffn_w_gate,
             L0_ffn_w_up, L0_ffn_w_down, L0_c, L0_wmod):
    x = np.asarray(L0_attn_data, dtype=np.float32)
    w1 = np.asarray(L0_attn_W1, dtype=np.float32)
    b1 = np.asarray(L0_attn_b1, dtype=np.float32)
    w2 = np.ascontiguousarray(np.asarray(L0_attn_W2, dtype=np.float32))
    wg = np.ascontiguousarray(np.asarray(L0_ffn_w_gate, dtype=np.float32))
    wu = np.ascontiguousarray(np.asarray(L0_ffn_w_up, dtype=np.float32))
    wd = np.ascontiguousarray(np.asarray(L0_ffn_w_down, dtype=np.float32))
    c = np.asarray(L0_c, dtype=np.float32)
    wmod = np.ascontiguousarray(np.asarray(L0_wmod, dtype=np.float32))

    w1b1 = np.ascontiguousarray(np.stack([w1.reshape(D), b1.reshape(D)],
                                         axis=1))
    cst = np.zeros((128, 8), np.float32)
    cst[:, 0] = np.float32(1.0 / 6.0)
    cst[:, 1] = np.float32(1e-12)
    cst[:, 2] = np.float32(1e-5)
    cst[:, 3] = np.float32(1e-12)
    ici = np.zeros((128, 2), np.int32)
    ici[:, 0] = np.int32(0x7FFFFFFF)
    eye = np.eye(128, dtype=np.float32)

    maps = []
    for i in range(NCORES):
        maps.append({
            "x": np.ascontiguousarray(x[2 * i:2 * i + 2].reshape(T, D)),
            "c2t": np.ascontiguousarray(c[2 * i:2 * i + 2].T),
            "w1b1": w1b1, "w2": w2, "wg": wg, "wu": wu, "wd": wd,
            "wmod": wmod, "cst": cst, "ici": ici, "eye": eye,
        })
    return maps


def _run(taps=(), core_ids=None, **inputs):
    nc = build(taps=taps)
    maps = _in_maps(**inputs)
    ids = list(range(NCORES)) if core_ids is None else core_ids
    res = run_bass_kernel_spmd(nc, maps[:len(ids)], core_ids=ids)
    return res


def kernel(**inputs):
    res = _run(**inputs)
    out = np.concatenate(
        [np.asarray(r["out"], np.float32).reshape(B, N, D)
         for r in res.results], axis=0)
    return out


# version 26
# speedup vs baseline: 1.0141x; 1.0141x over previous
import contextlib
import numpy as np
import concourse.bass as bass
import concourse.mybir as mybir
from concourse.bass_utils import run_bass_kernel_spmd

F32 = mybir.dt.float32
BF16 = mybir.dt.bfloat16
I32 = mybir.dt.int32
U8 = mybir.dt.uint8
AF = mybir.ActivationFunctionType
ALU = mybir.AluOpType
AX = mybir.AxisListType.X

NCORES = 8
B = 2
N = 2048
T = B * N
D = 1024
DI = 2816
KD = D // 128
KI = DI // 128
M = N // 128
CM = 12582912.0
PAIRS = ((0, 1024), (1024, 1024), (2048, 768))

ENG = {"sp": "sync", "pe": "tensor", "dv": "vector", "ac": "scalar",
       "pl": "gpsimd"}


class Rec:
    def __init__(self, nc, es):
        self.nc = nc
        self.es = es
        self.block = es.enter_context(nc.Block())
        self.sem = {e: es.enter_context(nc.semaphore("s_" + e)) for e in ENG}
        self.cnt = {e: 0 for e in ENG}
        self.water = {e: {} for e in ENG}
        self.isdma = set()
        self.ops = []

    def o(self, eng, emit, waits=(), inc=1, ctr=None):
        name = ctr or eng
        if name not in self.sem:
            self.sem[name] = self.es.enter_context(
                self.nc.semaphore("s_" + name))
            self.cnt[name] = 0
        ws = tuple(w for w in waits if w is not None)
        self.ops.append((eng, emit, ws, inc, name))
        self.cnt[name] += inc
        if inc >= 16:
            self.isdma.add((name, self.cnt[name]))
        return (name, self.cnt[name])

    def tok(self, eng):
        return (eng, self.cnt[eng])

    def flush(self):
        ops, self.ops = self.ops, []
        for e0 in ("sp", "pe", "dv", "ac", "pl"):
            chunk = [o for o in ops if o[0] == e0]
            if not chunk:
                continue

            def body(q, chunk=chunk, e0=e0):
                wat = self.water[e0]
                for (_e, emit, ws, inc, name) in chunk:
                    for w in ws:
                        if len(w) == 3:
                            we, wv, force = w
                        else:
                            (we, wv), force = w, False
                        if wv <= 0:
                            continue
                        if we == e0 and not force and \
                                (we, wv) not in self.isdma:
                            continue
                        if wv <= wat.get(we, 0) and not force:
                            continue
                        q.wait_ge(self.sem[we], wv)
                        if wv > wat.get(we, 0):
                            wat[we] = wv
                    inst = emit(q)
                    if inc:
                        inst.then_inc(self.sem[name], inc)

            getattr(self.block, ENG[e0])(body)


def F(t):
    return (t[0], t[1], True) if t is not None else None


def build(taps=()):
    nc = bass.Bass("TRN2", target_bir_lowering=False, debug=False)
    es = contextlib.ExitStack()

    x_in = nc.dram_tensor("x", [T, D], F32, kind="ExternalInput").ap()
    ct_in = nc.dram_tensor("c2t", [D, B], F32, kind="ExternalInput").ap()
    w1b1_in = nc.dram_tensor("w1b1", [D, 2], F32, kind="ExternalInput").ap()
    w2_in = nc.dram_tensor("w2", [2 * D, D], F32, kind="ExternalInput").ap()
    wg_in = nc.dram_tensor("wg", [DI, D], F32, kind="ExternalInput").ap()
    wu_in = nc.dram_tensor("wu", [DI, D], F32, kind="ExternalInput").ap()
    wd_in = nc.dram_tensor("wd", [D, DI], F32, kind="ExternalInput").ap()
    wmod_in = nc.dram_tensor("wmod", [D, 6 * D], F32, kind="ExternalInput").ap()
    cst_in = nc.dram_tensor("cst", [128, 8], F32, kind="ExternalInput").ap()
    ici_in = nc.dram_tensor("ici", [128, 2], I32, kind="ExternalInput").ap()
    eye_in = nc.dram_tensor("eye", [128, 128], F32, kind="ExternalInput").ap()
    out_dram = nc.dram_tensor("out", [T, D], F32, kind="ExternalOutput").ap()

    def dscr(name, shape, dt):
        kind = "ExternalOutput" if name in taps else "Internal"
        return nc.dram_tensor(name, shape, dt, kind=kind).ap()

    s_wgn = dscr("s_wgn", [DI, D], BF16)
    s_wun = dscr("s_wun", [DI, D], BF16)
    s_wdn = dscr("s_wdn", [D, DI], BF16)
    s_mod = dscr("s_mod", [B, 6 * D], F32)
    s_norm = [dscr("s_norm%d" % b, [N, D], BF16) for b in range(B)]
    s_xq = [dscr("s_xq%d" % b, [N, D], BF16) for b in range(B)]
    s_hq = dscr("s_hq", [N, DI], BF16)
    s_x1 = dscr("s_x1", [N, D], F32)
    s_flat = dscr("s_flat", [2, N], F32)
    s_cnt = dscr("s_cnt", [B, N], F32)
    s_dbg1 = dscr("s_dbg1", [128, D], F32)
    s_dbg2 = dscr("s_dbg2", [128, D], F32)
    s_dbg3 = dscr("s_dbg3", [128, D], F32)
    s_sv = dscr("s_sv", [B, N], F32)

    g = {}

    def sb(name, shape, dt):
        g[name] = es.enter_context(nc.sbuf_tensor("t_" + name, shape, dt))
        return g[name]

    sb("cst", [128, 8], F32)
    sb("ici", [128, 2], I32)
    sb("eye", [128, 128], F32)
    sb("ct", [128, 2 * KD], F32)
    sb("w1pp", [128, 2 * KD], F32)
    sb("ones", [1, 128], F32)
    for nm in ("sa", "ca", "ga", "cb", "sbv", "gb"):
        sb("r_" + nm, [128, D], F32)
    for i in range(2):
        sb("xr%d" % i, [128, D], F32)
    sb("x1r0", [128, D], F32)
    sb("x1r1", [128, D], F32)
    sb("p1n", [128, D], BF16)
    sb("deadb", [128, D], BF16)
    sb("st1", [128, 40], F32)
    sb("st2", [128, 32], F32)
    sb("svec", [128, M], F32)
    sb("cnt4", [128, 4], F32)
    sb("cntr", [128, M], F32)
    sb("srow", [16, 128], F32)
    sb("srow2", [16, 128], F32)
    sb("sBh", [128, D], BF16)
    sb("cBh", [128, D], F32)
    sb("cctz", [128, D], F32)
    sb("ccta", [128, D], F32)
    sb("mrow", [1, 512], F32)
    sb("q_src", [128, 2 * D], F32)
    sb("q_xn", [128, 2 * D], F32)
    sb("q_t", [128, D], I32)
    sb("q_mask", [128, D], U8)
    sb("q_m1", [128, 2 * D], F32)
    sb("q_qd", [128, 2 * D], BF16)
    sb("q_q", [128, 2 * D], BF16)
    sb("q_bm", [128, 128], F32)
    sb("q_sc", [128, 128], F32)
    sb("q_ri", [128, 128], F32)
    sb("silu_g", [128, D], F32)
    sb("hm", [128, D], F32)

    pool = es.enter_context(nc.sbuf_tensor("pool", [128, 96 * 1024], U8))

    def pview(off_kb, cols, dt):
        nb = mybir.dt.size(dt)
        return pool.ap()[:, off_kb * 1024: off_kb * 1024 + cols * nb].bitcast(dt)

    A1 = pview(0, KD * N, BF16)
    CCT = pview(32, KD * N, BF16)
    W2B = pview(64, 16 * D, BF16)
    XQT = pview(0, KD * N, BF16)
    GUP = [pview(32, 8 * D, BF16), pview(48, 8 * D, BF16)]
    UUP = [pview(64, 8 * D, BF16), pview(80, 8 * D, BF16)]
    WDT = pview(0, KI * D, BF16)
    HQT = pview(48, KI * D, BF16)
    WMS = [pview(0, 6 * D, F32), pview(24, 6 * D, F32)]
    MODEV = pview(60, 512, F32)

    R = Rec(nc, es)
    Tk = {}

    def tk(key):
        return Tk.get(key)

    def emit_repl_row(dram_row_ap, dst_ap, dst_w, ps_rep, evac_eng,
                      extra_src_wait=None, evac_kind="copy", sb_src=None):
        last = None
        nchunks = dst_w // 512
        for h in range(nchunks):
            hp = h % 2
            if sb_src is not None:
                tn, r0 = sb_src
                lt = R.o("sp", lambda q, h=h, tn=tn, r0=r0:
                         q.dma_start(
                             out=g["mrow"][0:1, 0:512],
                             in_=g[tn].ap()[r0 + 4 * h:r0 + 4 * h + 4, :]),
                         (extra_src_wait, tk("mrowf")), 16, ctr="mrowl")
            else:
                lt = R.o("sp", lambda q, h=h, dram_row_ap=dram_row_ap:
                         q.dma_start(
                             out=g["mrow"][0:1, 0:512],
                             in_=dram_row_ap[0:1, h * 512:(h + 1) * 512]),
                         (extra_src_wait, tk("mrowf")), 16, ctr="mrowl")
            mm = R.o("pe", lambda q, hp=hp: q.matmul(
                ps_rep[hp][:], g["ones"][:, 0:128],
                g["mrow"][0:1, 0:512], start=True,
                stop=True), (lt, Tk["ones"], tk("psr%d" % hp)))
            Tk["mrowf"] = mm
            if evac_kind == "plus1":
                ev = R.o(evac_eng, lambda q, h=h, hp=hp, dst_ap=dst_ap:
                         q.tensor_scalar(
                             dst_ap[:, h * 512:(h + 1) * 512], ps_rep[hp][:],
                             1.0, None, ALU.add), (mm,))
            elif evac_eng == "ac":
                ev = R.o("ac", lambda q, h=h, hp=hp, dst_ap=dst_ap:
                         q.activation(dst_ap[:, h * 512:(h + 1) * 512],
                                      ps_rep[hp][:], AF.Copy), (mm,))
            else:
                ev = R.o(evac_eng, lambda q, h=h, hp=hp, dst_ap=dst_ap:
                         q.tensor_copy(dst_ap[:, h * 512:(h + 1) * 512],
                                       ps_rep[hp][:]), (mm,))
            Tk["psr%d" % hp] = ev
            last = ev
            if sb_src is not None:
                Tk[("srowf", sb_src[0])] = lt
        return last

    def emit_gate_fix(dst, pre):
        w0 = (pre, tk("q_free_sel"), tk("q_free_st"),
              tk(("xnfree", 0)), tk(("m1free", 0)))
        gxn = g["q_xn"][:, 0:D]
        gm1 = g["q_m1"][:, 0:D]
        t1 = R.o("dv", lambda q, dst=dst: q.tensor_scalar(
            g["q_t"][:], dst.ap().bitcast(I32), g["ici"][:, 0:1], None,
            ALU.bitwise_and), w0)
        t1 = R.o("dv", lambda q: q.tensor_scalar(
            gm1, g["q_t"].ap().bitcast(F32), 0.25, None, ALU.max), ())
        t2 = R.o("dv", lambda q: q.tensor_scalar(
            gxn, gm1, -1.0, None, ALU.mult), ())
        t3 = R.o("dv", lambda q, dst=dst: q.tensor_scalar(
            g["q_mask"][:], dst[:], 0.0, None, ALU.is_lt), ())
        last = R.o("dv", lambda q, dst=dst: q.select(
            dst[:], g["q_mask"][:], gxn, gm1), ())
        Tk["q_free_gate"] = last
        Tk[("xnfree", 0)] = last
        Tk[("m1free", 0)] = last
        return last

    def emit_repl_mod(b, v, dst_name, kind, ps_rep):
        dst = g[dst_name]
        last = emit_repl_row(s_mod[b:b + 1, v * D:(v + 1) * D], dst.ap(), D,
                             ps_rep, "dv", extra_src_wait=Tk["mod_done"],
                             evac_kind=("plus1" if kind == "plus1"
                                        else "copy"))
        if kind == "gate":
            last = emit_gate_fix(dst, last)
            if b == 1 and dst_name == "r_ga" and "s_dbg2" in taps:
                R.o("ac", lambda q, dst=dst: q.dma_start(
                    out=s_dbg2, in_=dst[:]), (last,), 16, ctr="dbg")
            if b == 1 and dst_name == "r_gb" and "s_dbg3" in taps:
                R.o("ac", lambda q, dst=dst: q.dma_start(
                    out=s_dbg3, in_=dst[:]), (last,), 16, ctr="dbg")
        Tk[dst_name] = last

    qctr = [0]

    def emit_quant(src_ap, W, store_emit, pre_waits=(), store_waits=()):
        i = qctr[0] = qctr[0] + 1
        nb = W // 16
        par = (i % 2) * 64
        pw = (i % 2) * D
        bm = g["q_bm"][:, par:par + nb]
        sc = g["q_sc"][:, par:par + nb]
        ri = g["q_ri"][:, par:par + nb]
        xn = g["q_xn"][:, pw:pw + W]
        tq = g["q_t"][:, 0:W]
        msk = g["q_mask"][:, 0:W]
        m1 = g["q_m1"][:, pw:pw + W]
        qq = g["q_q"][:, pw:pw + W]
        qd = g["q_qd"][:, pw:pw + W]
        p2 = i % 2

        w0 = tuple(pre_waits) + (tk("q_free_gate"), Tk["consts"])
        t = R.o("dv", lambda q: q.reduce_max(
            bm, src_ap.rearrange("p (b s) -> p b s", s=16), axis=AX,
            apply_absolute_value=True), w0)
        t = R.o("dv", lambda q: q.tensor_scalar(
            sc, bm, g["cst"][:, 0:1], g["cst"][:, 1:2], ALU.mult, ALU.max),
            (F(t),))
        tri = R.o("dv", lambda q: q.reciprocal(ri, sc), (F(t),))
        txn = R.o("pl", lambda q: q.tensor_tensor(
            xn.rearrange("p (b s) -> p b s", s=16),
            src_ap.rearrange("p (b s) -> p b s", s=16),
            ri.unsqueeze(2).broadcast_to([128, nb, 16]), ALU.mult),
            (F(tri), tk(("xnfree", p2))) + tuple(pre_waits))
        Tk["q_xn_done"] = txn
        tam = R.o("dv", lambda q: q.tensor_scalar(
            tq, xn.bitcast(I32), g["ici"][:, 0:1], None, ALU.bitwise_and),
            (txn,), 0)
        tmask = R.o("dv", lambda q: q.tensor_scalar(
            msk, tq, 0x3F600000, None, ALU.is_gt), (), 0)
        t = R.o("dv", lambda q: q.tensor_scalar(
            tq, xn.bitcast(I32), 21, None, ALU.logical_shift_right), (), 0)
        t = R.o("dv", lambda q: q.tensor_scalar(tq, tq, 1, None, ALU.add),
                (), 0)
        thi = R.o("dv", lambda q: q.tensor_scalar(
            tq, tq, 1, 22, ALU.logical_shift_right, ALU.logical_shift_left),
            (), 0)
        tm = R.o("ac", lambda q: q.activation(m1, xn, AF.Copy, bias=CM,
                                              scale=2.0),
                 (txn, tk(("m1free", p2))))
        tm = R.o("ac", lambda q: q.activation(m1, m1, AF.Copy, bias=-CM / 2,
                                              scale=0.5), (F(tm),))
        Tk[("xnfree", p2)] = tm
        tsel = R.o("dv", lambda q: q.select(qq, msk, tq.bitcast(F32), m1),
                   (tm, tk(("qfree", p2))))
        Tk["q_free_sel"] = tsel
        Tk[("m1free", p2)] = tsel
        tqd = R.o("pl", lambda q: q.tensor_tensor(
            qd.rearrange("p (b s) -> p b s", s=16),
            qq.rearrange("p (b s) -> p b s", s=16),
            sc.unsqueeze(2).broadcast_to([128, nb, 16]), ALU.mult),
            (tsel, tk(("qdfree", p2))))
        Tk[("qfree", p2)] = tqd
        Tk["q_qd_done"] = tqd
        tst = R.o("ac", lambda q: store_emit(q, qd),
                  (tqd,) + tuple(store_waits), 16, ctr="qst")
        Tk[("qdfree", p2)] = tst
        Tk["q_free_st"] = tst
        return tst

    def ffn_quant_jobs():
        jobs = []
        for src, dst, nm in ((wg_in, s_wgn, "wg"), (wu_in, s_wun, "wu")):
            for c in range(KI):
                jobs.append((src, dst, nm, c, 0, D))
        for c in range(KD):
            for (off, W) in PAIRS:
                jobs.append((wd_in, s_wdn, "wd", c, off, W))
        for (j, (src, dst, nm, c, off, W)) in enumerate(jobs):
            par = j % 2
            qs = g["q_src"][:, par * D:par * D + W]
            lt = R.o("sp", lambda q, src=src, c=c, off=off, W=W, qs=qs:
                     q.dma_start(out=qs,
                                 in_=src[c * 128:(c + 1) * 128,
                                         off:off + W]),
                     (tk(("q_src_free", par)),), 16, ctr="wql")

            def store_emit(q, qd, dst=dst, c=c, off=off, W=W):
                return q.dma_start(
                    out=dst[c * 128:(c + 1) * 128, off:off + W], in_=qd)
            tst = emit_quant(qs, W, store_emit, pre_waits=(lt,))
            Tk[("q_src_free", par)] = Tk["q_xn_done"]
            Tk["wq_%s_%d" % (nm, c)] = tst
            yield

    ffnq = ffn_quant_jobs()
    ffnq_alive = [True]

    def ffnq_step(k=1):
        if not ffnq_alive[0]:
            return
        for _ in range(k):
            try:
                next(ffnq)
            except StopIteration:
                ffnq_alive[0] = False
                return

    def p1_chunk(b, m):
        slot = m % 2
        xr = g["xr%d" % slot]
        par = m % 2
        st = g["st1"]
        base = par * 20
        bno = st[:, base:base + 12]
        mean = st[:, base + 12:base + 13]
        var = st[:, base + 13:base + 14]
        rstd = st[:, base + 14:base + 15]
        nmr = st[:, base + 15:base + 16]
        sq2 = st[:, base + 16:base + 17]
        rn = st[:, base + 17:base + 18]
        row0 = b * N + m * 128

        if (b, m) in ((0, 0), (0, 1)):
            tl = Tk[("p1xl", m)]
        else:
            tl = R.o("sp", lambda q, xr=xr, row0=row0: q.dma_start(
                out=xr[:], in_=x_in[row0:row0 + 128, :]),
                (tk(("xrf", slot)),), 16, ctr="xr%d" % slot)
        pf = tk(("p1fin", par))
        t = R.o("dv", lambda q, xr=xr, bno=bno: q.bn_stats(
            bno[:, 0:6], xr[:, 0:512]), (tl, pf))
        tb = R.o("dv", lambda q, xr=xr, bno=bno: q.bn_stats(
            bno[:, 6:12], xr[:, 512:1024]), ())
        tg = R.o("dv", lambda q, bno=bno: q.bn_aggr(
            st[:, base + 12:base + 14], bno), (F(tb),))
        t = R.o("ac", lambda q, var=var, rstd=rstd: q.activation(
            rstd, var, AF.Ln, bias=g["cst"][:, 2:3]), (F(tg),))
        t = R.o("ac", lambda q, rstd=rstd: q.activation(
            rstd, rstd, AF.Exp, scale=-0.5), (F(t),))
        tnm = R.o("dv", lambda q, mean=mean, rstd=rstd, nmr=nmr:
                  q.tensor_scalar(nmr, mean, rstd, -1.0, ALU.mult,
                                  ALU.mult), (F(t),))
        tt1 = R.o("ac", lambda q, xr=xr, nmr=nmr, rstd=rstd: q.activation(
            xr[:], xr[:], AF.Identity, bias=nmr, scale=rstd),
            (F(tnm),))
        th = R.o("dv", lambda q, xr=xr: q.tensor_mul(
            xr[:], xr[:], g["r_ca"][:]), (F(tt1), Tk["r_ca"]))
        th = R.o("pl", lambda q, xr=xr: q.tensor_add(
            xr[:], xr[:], g["r_sa"][:]), (th, Tk["r_sa"]))
        ta2 = R.o("ac", lambda q, xr=xr, sq2=sq2: q.activation(
            g["deadb"][:], xr[:], AF.Square, accum_out=sq2), (th,))
        t = R.o("ac", lambda q, sq2=sq2, rn=rn: q.activation(
            rn, sq2, AF.Ln, bias=g["cst"][:, 3:4]), (F(ta2),))
        t = R.o("ac", lambda q, rn=rn, m=m: q.activation(
            g["svec"][:, m:m + 1], rn, AF.Exp, scale=0.5),
            (F(t), tk("svec_free")))
        t = R.o("ac", lambda q, rn=rn: q.activation(
            rn, rn, AF.Exp, scale=-0.5), (F(t),))
        tn = R.o("dv", lambda q, xr=xr, rn=rn: q.tensor_scalar(
            g["p1n"][:], xr[:], rn, None, ALU.mult),
            (F(t), tk("p1n_free")))
        Tk[("xrf", slot)] = tn
        ts = R.o("ac", lambda q, b=b, m=m: q.dma_start(
            out=s_norm[b][m * 128:(m + 1) * 128, :], in_=g["p1n"][:]),
            (tn,), 16, ctr="nrmst")
        Tk["p1n_free"] = ts
        Tk[("p1fin", par)] = tn
        if m == M - 1:
            Tk[("p1done", b)] = ts

    def emit_col_to_dram(src_col_ap, dram_row, ps_t16, srctok, tn):
        t = R.o("pe", lambda q, src_col_ap=src_col_ap: q.transpose(
            ps_t16[0:M, :], src_col_ap, g["eye"][:]),
            (srctok, Tk["consts"], tk("pst16")))
        t = R.o("dv", lambda q, tn=tn: q.tensor_copy(
            g[tn][0:M, :], ps_t16[0:M, :]),
            (t, tk(("srowf", tn)), tk(("srowfl", tn))))
        Tk["pst16"] = t
        tsv = t
        t = R.o("pl", lambda q, dram_row=dram_row, tn=tn: q.dma_start(
            out=dram_row.rearrange("a (p f) -> (a p) f", p=M),
            in_=g[tn].ap()[0:M, :]),
            (t,), 16, ctr="fl")
        Tk[("srowfl", tn)] = t
        return t, tsv

    t = R.o("sp", lambda q: q.dma_start(out=g["cst"][:], in_=cst_in), (), 16,
            ctr="cstl")
    t = R.o("sp", lambda q: q.dma_start(out=g["ici"][:], in_=ici_in), (), 16,
            ctr="cstl")
    t = R.o("sp", lambda q: q.dma_start(out=g["eye"][:], in_=eye_in), (), 16,
            ctr="cstl")
    t = R.o("sp", lambda q: q.dma_start(
        out=g["ct"].ap().rearrange("p (k c) -> p k c", c=B),
        in_=ct_in.rearrange("(k p) c -> p k c", p=128)), (), 16, ctr="cstl")
    t = R.o("sp", lambda q: q.dma_start(
        out=g["w1pp"].ap().rearrange("p (k c) -> p k c", c=2),
        in_=w1b1_in.rearrange("(k p) c -> p k c", p=128)), (), 16, ctr="cstl")
    Tk["consts"] = t
    Tk["ones"] = R.o("dv", lambda q: q.memset(g["ones"][:], 1.0), ())
    Tk["ctsilu"] = R.o("ac", lambda q: q.activation(
        g["ct"][:], g["ct"][:], AF.Silu), (Tk["consts"],))

    with nc.psum_tensor("psma0", [128, 512], F32) as pa0, \
         nc.psum_tensor("psma1", [128, 512], F32) as pa1, \
         nc.psum_tensor("psma2", [128, 512], F32) as pa2, \
         nc.psum_tensor("psma3", [128, 512], F32) as pa3, \
         nc.psum_tensor("psr0a", [128, 512], F32) as pr0, \
         nc.psum_tensor("psr1a", [128, 512], F32) as pr1:
        pma = [pa0, pa1, pa2, pa3]
        ps_rep = [pr0, pr1]
        slab_tok = [None, None]
        for pm in range(2):
            Tk[("p1xl", pm)] = R.o(
                "sp", lambda q, pm=pm: q.dma_start(
                    out=g["xr%d" % pm][:],
                    in_=x_in[pm * 128:(pm + 1) * 128, :]), (), 16,
                ctr="xr%d" % pm)
        mm = None
        for k in range(KD):
            sl = k % 2
            lt = R.o("sp", lambda q, k=k, sl=sl: q.dma_start(
                out=WMS[sl][:], in_=wmod_in[k * 128:(k + 1) * 128, :]),
                (slab_tok[sl],), 16, ctr="wm%d" % sl)
            for n in range(12):
                po = 32 * (n % 3)
                mm = R.o("pe", lambda q, k=k, n=n, sl=sl, po=po: q.matmul(
                    pma[n // 3][po:po + 2, :],
                    g["ct"][:, 2 * k:2 * k + 2],
                    WMS[sl][:, n * 512:(n + 1) * 512],
                    start=(k == 0), stop=(k == KD - 1)),
                    (lt, Tk["ctsilu"]))
            slab_tok[sl] = mm
            ffnq_step(1)
        Tk["mod_mm_done"] = mm
        tmst = None
        for bk in range(4):
            tev = R.o("dv", lambda q, bk=bk: q.tensor_copy(
                MODEV[0:66, :], pma[bk][0:66, :]), (mm, tmst))
            for c in range(B):
                tmst = R.o(
                    "ac", lambda q, bk=bk, c=c: q.dma_start(
                        out=s_mod[c:c + 1, 3 * bk * 512:(3 * bk + 3) * 512]
                        .rearrange("a (n w) -> (a n) w", w=512),
                        in_=MODEV[c:c + 65:32, :]),
                    (tev,), 16, ctr="modst")
        Tk["mod_done"] = tmst

        emit_repl_mod(0, 0, "r_sa", "plain", ps_rep)
        emit_repl_mod(0, 1, "r_ca", "plus1", ps_rep)
        emit_repl_mod(0, 2, "r_ga", "gate", ps_rep)
        emit_repl_mod(0, 3, "r_sbv", "plain", ps_rep)
        emit_repl_mod(0, 4, "r_cb", "plus1", ps_rep)
        emit_repl_mod(0, 5, "r_gb", "gate", ps_rep)

        Tk["w2pre"] = R.o("pl", lambda q: q.dma_start(
            out=W2B.rearrange("p (k d) -> p k d", d=D),
            in_=w2_in.rearrange("(k p) d -> p k d", p=128)),
            (Tk["mod_mm_done"],), 16, ctr="w2l")
        for m in range(M):
            p1_chunk(0, m)
            ffnq_step(2)
        R.flush()

    for b in range(B):
        with nc.psum_tensor("ps_sim%d" % b, [128, 4 * 512], F32) as ps_sim, \
             nc.psum_tensor("pst16%d" % b, [16, 128], F32) as ps_t16, \
             nc.psum_tensor("psr0b%d" % b, [128, 512], F32) as pr0, \
             nc.psum_tensor("psr1b%d" % b, [128, 512], F32) as pr1:
            ps_rep = [pr0, pr1]
            for k in range(KD):
                Tk[("A1", k)] = R.o("sp", lambda q, k=k, b=b:
                          q.dma_start_transpose(
                    A1[:, k * N:(k + 1) * N],
                    s_norm[b][:, k * 128:(k + 1) * 128]),
                    (tk(("p1done", b)), tk("pool_lo_free"),
                     tk("pool_hi_free"), Tk["mod_mm_done"]), 16, ctr="a1l")

            if b == 1:
                emit_repl_mod(1, 2, "r_ga", "gate", ps_rep)
                emit_repl_mod(1, 3, "r_sbv", "plain", ps_rep)
                emit_repl_mod(1, 4, "r_cb", "plus1", ps_rep)
                emit_repl_mod(1, 5, "r_gb", "gate", ps_rep)

            tfl, tsv = emit_col_to_dram(g["svec"][:, 0:M], s_flat[0:1, :],
                                        ps_t16, tk(("p1done", b)),
                                        "srow2")
            Tk["svec_free"] = tsv
            if "s_sv" in taps:
                R.o("pl", lambda q, b=b: q.dma_start(
                    out=s_sv[b:b + 1, :], in_=s_flat[0:1, :]), (tfl,), 16, ctr="fl")

            mmlast = None
            for n in range(4):
                Tk[("simev", -1, n)] = tk(("simev", M - 1, n))
            for m in range(M):
                for k in range(KD):
                    for n in range(4):
                        w = [Tk[("A1", k)]]
                        if k == 0:
                            w.append(tk(("simev", m - 1, n)))
                        mmlast = R.o("pe", lambda q, k=k, n=n, m=m: q.matmul(
                            ps_sim[:, n * 512:(n + 1) * 512],
                            A1[:, k * N + m * 128:k * N + m * 128 + 128],
                            A1[:, k * N + n * 512:k * N + (n + 1) * 512],
                            start=(k == 0), stop=(k == KD - 1)), w)
                Tk[("simmm", m)] = mmlast
                for n in (0,):
                    Tk[("simev", m, n)] = R.o(
                        "dv", lambda q, n=n: q.tensor_scalar(
                            g["deadb"][:, 0:512],
                            ps_sim[:, n * 512:(n + 1) * 512], 0.0, None,
                            ALU.max, ALU.add,
                            accum_out=g["cnt4"][:, n:n + 1]),
                        (Tk[("simmm", m)],))
                for n in (1, 2, 3):
                    Tk[("simev", m, n)] = R.o(
                        "ac", lambda q, n=n: q.activation(
                            g["deadb"][:, 512:1024],
                            ps_sim[:, n * 512:(n + 1) * 512], AF.Relu,
                            accum_out=g["cnt4"][:, n:n + 1]),
                        (Tk[("simmm", m)], tk(("cnt", m - 1))))
                Tk[("cnt", m)] = R.o("dv", lambda q, m=m: q.reduce_sum(
                    g["cntr"][:, m:m + 1], g["cnt4"][:, 0:4], axis=AX),
                    (F(Tk[("simev", m, 0)]),
                     Tk[("simev", m, 3)]))
                if b == 0:
                    ffnq_step(2)

            tfc, tcv = emit_col_to_dram(g["cntr"][:, 0:M], s_flat[1:2, :],
                                        ps_t16, F(Tk[("cnt", M - 1)]),
                                        "srow")
            if "s_cnt" in taps:
                R.o("pl", lambda q, b=b: q.dma_start(
                    out=s_cnt[b:b + 1, :], in_=s_flat[1:2, :]), (tfc,), 16, ctr="fl")

            for half in range(2):
                hs = half * 1024
                tcb = emit_repl_row(None, g["cBh"].ap(),
                                    1024, ps_rep, "dv", extra_src_wait=tcv,
                                    sb_src=("srow", 8 * half))
                tsb = emit_repl_row(None, g["sBh"].ap(),
                                    1024, ps_rep, "ac", extra_src_wait=tsv,
                                    sb_src=("srow2", 8 * half))
                for k in range(KD):
                    tz = R.o("dv", lambda q, k=k: q.tensor_scalar(
                        g["cctz"][:], g["cBh"][:],
                        g["w1pp"][:, 2 * k:2 * k + 1],
                        g["w1pp"][:, 2 * k + 1:2 * k + 2],
                        ALU.mult, ALU.add), (tcb, tk("cct_free")))
                    t = R.o("dv", lambda q: q.tensor_scalar(
                        g["ccta"].ap().bitcast(I32),
                        g["cctz"].ap().bitcast(I32),
                        g["ici"][:, 0:1], None, ALU.bitwise_and),
                        (F(tz), tk("ccta_free")))
                    t = R.o("ac", lambda q: q.activation(
                        g["ccta"][:], g["ccta"][:], AF.Exp, scale=-1.0),
                        (t,))
                    t = R.o("ac", lambda q: q.activation(
                        g["ccta"][:], g["ccta"][:], AF.Ln, bias=1.0),
                        (F(t),))
                    t = R.o("dv", lambda q, k=k, hs=hs:
                            q.scalar_tensor_tensor(
                        CCT[:, k * N + hs:k * N + hs + 1024], g["cctz"][:],
                        0.0, g["ccta"][:], ALU.max, ALU.add),
                        (t, tk("pool_lo_free"), Tk[("simmm", M - 1)]))
                    Tk["cct_free"] = t
                    Tk["ccta_free"] = t
                    Tk[("CCT", k, half)] = t
                for k in range(KD):
                    Tk[("hT", k, half)] = R.o(
                        "dv", lambda q, k=k, hs=hs: q.tensor_mul(
                            A1[:, k * N + hs:k * N + hs + 1024],
                            A1[:, k * N + hs:k * N + hs + 1024],
                            g["sBh"][:]), (tsb, Tk[("simmm", M - 1)]))
            R.flush()

        with nc.psum_tensor("ps_at0%d" % b, [128, D], F32) as pa0, \
             nc.psum_tensor("ps_at1%d" % b, [128, D], F32) as pa1, \
             nc.psum_tensor("ps_at2%d" % b, [128, D], F32) as pa2, \
             nc.psum_tensor("psr0c%d" % b, [128, 512], F32) as pr0, \
             nc.psum_tensor("psr1c%d" % b, [128, 512], F32) as pr1:
            ps_at = [pa0, pa1, pa2]
            ps_rep = [pr0, pr1]
            tw2 = Tk["w2pre"]

            if b == 0:
                emit_repl_mod(1, 0, "r_sa", "plain", ps_rep)
                emit_repl_mod(1, 1, "r_ca", "plus1", ps_rep)

            mmlast = None
            Tk[("hmfw2", 0)] = tk("hm_free")
            Tk[("hmfw2", 1)] = tk(("x1rf", 0))
            for dd in range(1, 4):
                Tk[("atev", -dd)] = tk(("atev", M - dd))
            for m in range(M):
                slot = m % 2
                xr = g["xr%d" % slot]
                row0 = b * N + m * 128
                tl = R.o("sp", lambda q, xr=xr, row0=row0: q.dma_start(
                    out=xr[:], in_=x_in[row0:row0 + 128, :]),
                    (tk(("xrf", slot)), tk(("x1stp", slot))), 16,
                    ctr="xr%d" % slot)
                half = m // 8
                for k in range(16):
                    kk = k % 8
                    lhs = A1 if k < 8 else CCT
                    w = [Tk[("hT", kk, half)] if k < 8
                         else Tk[("CCT", kk, half)], tw2]
                    if k == 0:
                        w.append(tk(("atev", m - 3)))
                    for n in range(2):
                        mmlast = R.o("pe", lambda q, k=k, kk=kk, n=n, m=m,
                                     lhs=lhs: q.matmul(
                            ps_at[m % 3][:, n * 512:(n + 1) * 512],
                            lhs[:, kk * N + m * 128:kk * N + m * 128 + 128],
                            W2B[:, k * D + n * 512:k * D + (n + 1) * 512],
                            start=(k == 0), stop=(k == 15)), w)
                Tk[("atmm", m)] = mmlast
                x1t = xr
                t1 = R.o("dv", lambda q, m=m: q.tensor_mul(
                    g["silu_g"][:], ps_at[m % 3][:], g["r_ga"][:]),
                    (Tk[("atmm", m)], Tk["r_ga"], tk("t1_free")))
                Tk[("atev", m)] = t1
                tx1 = R.o("pl", lambda q, x1t=x1t: q.tensor_add(
                    x1t[:], g["silu_g"][:], x1t[:]), (t1, tl))
                Tk["t1_free"] = tx1
                tsx = R.o("ac", lambda q, x1t=x1t, m=m: q.dma_start(
                    out=s_x1[m * 128:(m + 1) * 128, :], in_=x1t[:]),
                    (tx1, tk("s_x1_free")), 16, ctr="x1st%d" % (m % 2))
                Tk[("s_x1", m)] = tsx
                Tk[("x1stp", m % 2)] = tsx

                par = m % 2
                st = g["st2"]
                base = par * 16
                bno = st[:, base:base + 12]
                mean = st[:, base + 12:base + 13]
                var = st[:, base + 13:base + 14]
                rstd = st[:, base + 14:base + 15]
                nmr = st[:, base + 15:base + 16]
                lf = tk(("ln2fin", par))
                t = R.o("dv", lambda q, x1t=x1t, bno=bno: q.bn_stats(
                    bno[:, 0:6], x1t[:, 0:512]), (tx1, lf))
                tb = R.o("dv", lambda q, x1t=x1t, bno=bno: q.bn_stats(
                    bno[:, 6:12], x1t[:, 512:1024]), ())
                tg = R.o("dv", lambda q, base=base: q.bn_aggr(
                    st[:, base + 12:base + 14], st[:, base:base + 12]),
                    (F(tb),))
                t = R.o("ac", lambda q, var=var, rstd=rstd: q.activation(
                    rstd, var, AF.Ln, bias=g["cst"][:, 2:3]), (F(tg),))
                t = R.o("ac", lambda q, rstd=rstd: q.activation(
                    rstd, rstd, AF.Exp, scale=-0.5), (F(t),))
                t = R.o("dv", lambda q, mean=mean, rstd=rstd, nmr=nmr:
                        q.tensor_scalar(nmr, mean, rstd, -1.0, ALU.mult,
                                        ALU.mult), (F(t),))
                hmw = g["hm"] if par == 0 else g["x1r0"]
                th2 = R.o("ac", lambda q, x1t=x1t, nmr=nmr, rstd=rstd,
                          hmw=hmw: q.activation(hmw[:], x1t[:], AF.Identity,
                                                bias=nmr, scale=rstd),
                          (F(t), tk(("hmfw2", par))))
                Tk[("ln2fin", par)] = th2
                Tk[("xrf", slot)] = th2
                th2 = R.o("dv", lambda q, hmw=hmw: q.tensor_mul(
                    hmw[:], hmw[:], g["r_cb"][:]),
                    (th2, Tk["r_cb"]))
                th2 = R.o("pl", lambda q, hmw=hmw: q.tensor_add(
                    hmw[:], hmw[:], g["r_sbv"][:]),
                    (th2, Tk["r_sbv"]))

                def store_xq(q, qd, b=b, m=m):
                    return q.dma_start(
                        out=s_xq[b][m * 128:(m + 1) * 128, :], in_=qd)
                tst = emit_quant(hmw[:], D, store_xq, pre_waits=(th2,))
                Tk[("hmfw2", par)] = Tk["q_xn_done"]
                ffnq_step(1)
            Tk["A1_lastuse"] = Tk[("atmm", M - 1)]
            Tk["xq_done"] = Tk["q_free_st"]
            Tk["hm_free"] = Tk[("hmfw2", 0)]
            R.flush()

        with nc.psum_tensor("ps_g0%d" % b, [128, D], F32) as pg0, \
             nc.psum_tensor("ps_g1%d" % b, [128, D], F32) as pg1, \
             nc.psum_tensor("ps_u0%d" % b, [128, D], F32) as pu0, \
             nc.psum_tensor("ps_u1%d" % b, [128, D], F32) as pu1:
            ps_g = [pg0, pg1]
            ps_u = [pu0, pu1]
            for k in range(KD):
                Tk[("XQT", k)] = R.o("sp", lambda q, k=k, b=b:
                          q.dma_start_transpose(
                    XQT[:, k * N:(k + 1) * N],
                    s_xq[b][:, k * 128:(k + 1) * 128]),
                    (Tk["xq_done"], Tk["A1_lastuse"]), 16, ctr="xqtl")

            p1_left = list(range(M)) if b == 0 else []
            it = 0
            Tk[("hmfree", 0)] = tk("hm_free")
            Tk[("guev", -2)] = tk(("guev", 3 * M - 2))
            Tk[("guev", -1)] = tk(("guev", 3 * M - 1))
            for (p, (poff, Wp)) in enumerate(PAIRS):
                pp = p % 2
                wq_wait = Tk["wq_wu_%d" % (KI - 1)]
                for k in range(KD):
                    tgl = R.o(
                        "sp", lambda q, k=k, pp=pp, poff=poff, Wp=Wp:
                        q.dma_start_transpose(
                            GUP[pp][:, k * D:k * D + Wp],
                            s_wgn[poff:poff + Wp, k * 128:(k + 1) * 128]),
                        (wq_wait, tk(("gup_free", pp)), Tk["A1_lastuse"],
                         tk("pool_hi_free")), 16, ctr="gupl%d" % pp)
                    tul = R.o(
                        "sp", lambda q, k=k, pp=pp, poff=poff, Wp=Wp:
                        q.dma_start_transpose(
                            UUP[pp][:, k * D:k * D + Wp],
                            s_wun[poff:poff + Wp, k * 128:(k + 1) * 128]),
                        (tk(("uup_free", pp)), Tk["A1_lastuse"],
                         tk("pool_hi_free")), 16, ctr="uupl%d" % pp)
                    Tk[("gup", pp, k)] = tgl
                    Tk[("uup", pp, k)] = tul
                nh = (Wp + 511) // 512
                for m in range(M):
                    itp = it % 2
                    mmlast = None
                    for k in range(KD):
                        w = [Tk[("XQT", k)], Tk[("gup", pp, k)],
                             Tk[("uup", pp, k)]]
                        if k == 0:
                            w.append(tk(("guev", it - 2)))
                        for n in range(nh):
                            c0 = n * 512
                            c1 = min(Wp, (n + 1) * 512)
                            R.o("pe", lambda q, k=k, m=m, pp=pp, c0=c0,
                                c1=c1, itp=itp: q.matmul(
                                ps_g[itp][:, c0:c1],
                                XQT[:, k * N + m * 128:k * N + m * 128 + 128],
                                GUP[pp][:, k * D + c0:k * D + c1],
                                start=(k == 0), stop=(k == KD - 1)), w)
                            mmlast = R.o("pe", lambda q, k=k, m=m, pp=pp,
                                         c0=c0, c1=c1, itp=itp: q.matmul(
                                ps_u[itp][:, c0:c1],
                                XQT[:, k * N + m * 128:k * N + m * 128 + 128],
                                UUP[pp][:, k * D + c0:k * D + c1],
                                start=(k == 0), stop=(k == KD - 1)), ())
                    Tk[("gumm", it)] = mmlast
                    hmv = (g["hm"] if itp == 0 else g["q_src"])[:, 0:Wp]
                    hw = [None, tk(("hmfree", itp))]
                    if it == 1:
                        hw += [tk(("q_src_free", 0)), tk(("q_src_free", 1))]
                    tsg = R.o("ac", lambda q, Wp=Wp, itp=itp: q.activation(
                        g["silu_g"][:, 0:Wp], ps_g[itp][:, 0:Wp], AF.Silu),
                        (Tk[("gumm", it)], tk("t1_free")))
                    hw[0] = tsg
                    thm = R.o("dv", lambda q, Wp=Wp, itp=itp, hmv=hmv:
                              q.tensor_mul(hmv, g["silu_g"][:, 0:Wp],
                                           ps_u[itp][:, 0:Wp]), tuple(hw))
                    Tk["t1_free"] = thm
                    Tk[("guev", it)] = thm

                    def store_hq(q, qd, m=m, poff=poff, Wp=Wp):
                        return q.dma_start(
                            out=s_hq[m * 128:(m + 1) * 128, poff:poff + Wp],
                            in_=qd)
                    tst = emit_quant(hmv, Wp, store_hq,
                                     pre_waits=(thm,),
                                     store_waits=(tk("s_hq_free"),))
                    Tk[("hmfree", itp)] = Tk["q_xn_done"]
                    Tk[("hq", p, m)] = tst
                    it += 1
                Tk[("gup_free", pp)] = Tk[("gumm", it - 1)]
                Tk[("uup_free", pp)] = Tk[("gumm", it - 1)]
            Tk["hq_done"] = Tk[("hq", 2, M - 1)]
            Tk["gu_lastmm"] = Tk[("gumm", it - 1)]
            Tk["hm_free"] = Tk[("hmfree", 0)]
            R.flush()

        with nc.psum_tensor("ps_d0%d" % b, [128, D], F32) as pd0, \
             nc.psum_tensor("ps_d1%d" % b, [128, D], F32) as pd1, \
             nc.psum_tensor("ps_d2%d" % b, [128, D], F32) as pd2, \
             nc.psum_tensor("ps_d3%d" % b, [128, D], F32) as pd3:
            ps_d = [pd0, pd1, pd2, pd3]
            for ki in range(KI):
                twd = R.o("sp", lambda q, ki=ki:
                          q.dma_start_transpose(
                    WDT[:, ki * D:(ki + 1) * D],
                    s_wdn[:, ki * 128:(ki + 1) * 128]),
                    (Tk["wq_wd_%d" % (KD - 1)], Tk["gu_lastmm"],
                     Tk["xq_done"]), 16, ctr="wdtl")
                Tk[("WDT", ki)] = twd
            gbtok = Tk["r_gb"]
            for dd in range(1, 5):
                Tk[("dnev", -dd)] = tk(("dnev", M - dd))

            mmlast = None
            for half in range(2):
                for ki in range(KI):
                    thq = R.o("sp", lambda q, ki=ki, half=half:
                              q.dma_start_transpose(
                        HQT[:, ki * D:(ki + 1) * D],
                        s_hq[half * 1024:(half + 1) * 1024,
                             ki * 128:(ki + 1) * 128]),
                        (Tk["hq_done"], Tk["gu_lastmm"],
                         tk(("hqt_used", ki))), 16, ctr="hqtl")
                    Tk[("HQT", ki)] = thq
                for mm_i in range(8):
                    m = half * 8 + mm_i
                    x1r = g["x1r0" if m % 2 == 0 else "x1r1"]
                    tl = R.o("sp", lambda q, x1r=x1r, m=m: q.dma_start(
                        out=x1r[:], in_=s_x1[m * 128:(m + 1) * 128, :]),
                        (Tk[("s_x1", m)], tk(("x1rf", m % 2)))
                        + ((tk("xq_done"),) if m == 0 else ()), 16,
                        ctr="x1r%d" % (m % 2))
                    for ki in range(KI):
                        w = [Tk[("WDT", ki)], Tk[("HQT", ki)]]
                        if ki == 0:
                            w.append(tk(("dnev", m - 4)))
                        for n in range(2):
                            mmlast = R.o("pe", lambda q, ki=ki, m=m,
                                         mm_i=mm_i, n=n: q.matmul(
                                ps_d[m % 4][:, n * 512:(n + 1) * 512],
                                HQT[:, ki * D + mm_i * 128:
                                    ki * D + mm_i * 128 + 128],
                                WDT[:, ki * D + n * 512:
                                    ki * D + (n + 1) * 512],
                                start=(ki == 0), stop=(ki == KI - 1)), w)
                        Tk[("hqt_used", ki)] = mmlast
                    Tk[("dnmm", m)] = mmlast
                    t3 = R.o("dv", lambda q, m=m: q.tensor_mul(
                        g["silu_g"][:], ps_d[m % 4][:], g["r_gb"][:]),
                        (Tk[("dnmm", m)], gbtok, tk("t1_free")))
                    Tk[("dnev", m)] = t3
                    to = R.o("pl", lambda q, x1r=x1r:
                             q.tensor_add(x1r[:], g["silu_g"][:], x1r[:]),
                             (t3, tl))
                    Tk["t1_free"] = to
                    tso = R.o("ac", lambda q, x1r=x1r, m=m, b=b:
                              q.dma_start(
                        out=out_dram[b * N + m * 128:
                                     b * N + (m + 1) * 128, :],
                        in_=x1r[:]), (to,), 16, ctr="outst%d" % (m % 2))
                    Tk[("x1rf", m % 2)] = tso
                    Tk[("outst", b, m)] = tso
                    if b == 0:
                        p1_chunk(1, m)
            Tk["s_x1_free"] = tk(("x1rf", 1))
            Tk["s_hq_free"] = Tk[("dnmm", M - 1)]
            Tk["pool_lo_free"] = Tk[("dnmm", M - 1)]
            Tk["pool_hi_free"] = Tk[("dnmm", M - 1)]
            if b == 0:
                Tk["w2pre"] = R.o("pl", lambda q: q.dma_start(
                    out=W2B.rearrange("p (k d) -> p k d", d=D),
                    in_=w2_in.rearrange("(k p) d -> p k d", p=128)),
                    (Tk["pool_hi_free"],), 16, ctr="w2l")
            R.flush()

    v0 = R.cnt["outst0"]
    v1 = R.cnt["outst1"]
    R.o("sp", lambda q: q.wait_ge(R.sem["outst0"], v0), (), 0)
    R.o("sp", lambda q: q.wait_ge(R.sem["outst1"], v1), (), 0)
    R.flush()

    es.close()
    return nc


def _in_maps(L0_attn_data, L0_attn_W1, L0_attn_b1, L0_attn_W2, L0_ffn_w_gate,
             L0_ffn_w_up, L0_ffn_w_down, L0_c, L0_wmod):
    x = np.asarray(L0_attn_data, dtype=np.float32)
    w1 = np.asarray(L0_attn_W1, dtype=np.float32)
    b1 = np.asarray(L0_attn_b1, dtype=np.float32)
    w2 = np.ascontiguousarray(np.asarray(L0_attn_W2, dtype=np.float32))
    wg = np.ascontiguousarray(np.asarray(L0_ffn_w_gate, dtype=np.float32))
    wu = np.ascontiguousarray(np.asarray(L0_ffn_w_up, dtype=np.float32))
    wd = np.ascontiguousarray(np.asarray(L0_ffn_w_down, dtype=np.float32))
    c = np.asarray(L0_c, dtype=np.float32)
    wmod = np.ascontiguousarray(np.asarray(L0_wmod, dtype=np.float32))

    w1b1 = np.ascontiguousarray(np.stack([w1.reshape(D), b1.reshape(D)],
                                         axis=1))
    cst = np.zeros((128, 8), np.float32)
    cst[:, 0] = np.float32(1.0 / 6.0)
    cst[:, 1] = np.float32(1e-12)
    cst[:, 2] = np.float32(1e-5)
    cst[:, 3] = np.float32(1e-12)
    ici = np.zeros((128, 2), np.int32)
    ici[:, 0] = np.int32(0x7FFFFFFF)
    eye = np.eye(128, dtype=np.float32)

    maps = []
    for i in range(NCORES):
        maps.append({
            "x": np.ascontiguousarray(x[2 * i:2 * i + 2].reshape(T, D)),
            "c2t": np.ascontiguousarray(c[2 * i:2 * i + 2].T),
            "w1b1": w1b1, "w2": w2, "wg": wg, "wu": wu, "wd": wd,
            "wmod": wmod, "cst": cst, "ici": ici, "eye": eye,
        })
    return maps


def _run(taps=(), core_ids=None, **inputs):
    nc = build(taps=taps)
    maps = _in_maps(**inputs)
    ids = list(range(NCORES)) if core_ids is None else core_ids
    res = run_bass_kernel_spmd(nc, maps[:len(ids)], core_ids=ids)
    return res


def kernel(**inputs):
    res = _run(**inputs)
    out = np.concatenate(
        [np.asarray(r["out"], np.float32).reshape(B, N, D)
         for r in res.results], axis=0)
    return out


# version 27
# speedup vs baseline: 1.0149x; 1.0008x over previous
import contextlib
import numpy as np
import concourse.bass as bass
import concourse.mybir as mybir
from concourse.bass_utils import run_bass_kernel_spmd

F32 = mybir.dt.float32
BF16 = mybir.dt.bfloat16
I32 = mybir.dt.int32
U8 = mybir.dt.uint8
AF = mybir.ActivationFunctionType
ALU = mybir.AluOpType
AX = mybir.AxisListType.X

NCORES = 8
B = 2
N = 2048
T = B * N
D = 1024
DI = 2816
KD = D // 128
KI = DI // 128
M = N // 128
CM = 12582912.0
PAIRS = ((0, 1024), (1024, 1024), (2048, 768))

ENG = {"sp": "sync", "pe": "tensor", "dv": "vector", "ac": "scalar",
       "pl": "gpsimd"}


class Rec:
    def __init__(self, nc, es):
        self.nc = nc
        self.es = es
        self.block = es.enter_context(nc.Block())
        self.sem = {e: es.enter_context(nc.semaphore("s_" + e)) for e in ENG}
        self.cnt = {e: 0 for e in ENG}
        self.water = {e: {} for e in ENG}
        self.isdma = set()
        self.ops = []

    def o(self, eng, emit, waits=(), inc=1, ctr=None):
        name = ctr or eng
        if name not in self.sem:
            self.sem[name] = self.es.enter_context(
                self.nc.semaphore("s_" + name))
            self.cnt[name] = 0
        ws = tuple(w for w in waits if w is not None)
        self.ops.append((eng, emit, ws, inc, name))
        self.cnt[name] += inc
        if inc >= 16:
            self.isdma.add((name, self.cnt[name]))
        return (name, self.cnt[name])

    def tok(self, eng):
        return (eng, self.cnt[eng])

    def flush(self):
        ops, self.ops = self.ops, []
        for e0 in ("sp", "pe", "dv", "ac", "pl"):
            chunk = [o for o in ops if o[0] == e0]
            if not chunk:
                continue

            def body(q, chunk=chunk, e0=e0):
                wat = self.water[e0]
                for (_e, emit, ws, inc, name) in chunk:
                    for w in ws:
                        if len(w) == 3:
                            we, wv, force = w
                        else:
                            (we, wv), force = w, False
                        if wv <= 0:
                            continue
                        if we == e0 and not force and \
                                (we, wv) not in self.isdma:
                            continue
                        if wv <= wat.get(we, 0) and not force:
                            continue
                        q.wait_ge(self.sem[we], wv)
                        if wv > wat.get(we, 0):
                            wat[we] = wv
                    inst = emit(q)
                    if inc:
                        inst.then_inc(self.sem[name], inc)

            getattr(self.block, ENG[e0])(body)


def F(t):
    return (t[0], t[1], True) if t is not None else None


def build(taps=()):
    nc = bass.Bass("TRN2", target_bir_lowering=False, debug=False)
    es = contextlib.ExitStack()

    x_in = nc.dram_tensor("x", [T, D], F32, kind="ExternalInput").ap()
    ct_in = nc.dram_tensor("c2t", [D, B], F32, kind="ExternalInput").ap()
    w1b1_in = nc.dram_tensor("w1b1", [D, 2], F32, kind="ExternalInput").ap()
    w2_in = nc.dram_tensor("w2", [2 * D, D], F32, kind="ExternalInput").ap()
    wg_in = nc.dram_tensor("wg", [DI, D], F32, kind="ExternalInput").ap()
    wu_in = nc.dram_tensor("wu", [DI, D], F32, kind="ExternalInput").ap()
    wd_in = nc.dram_tensor("wd", [D, DI], F32, kind="ExternalInput").ap()
    wmod_in = nc.dram_tensor("wmod", [D, 6 * D], F32, kind="ExternalInput").ap()
    cst_in = nc.dram_tensor("cst", [128, 8], F32, kind="ExternalInput").ap()
    ici_in = nc.dram_tensor("ici", [128, 2], I32, kind="ExternalInput").ap()
    eye_in = nc.dram_tensor("eye", [128, 128], F32, kind="ExternalInput").ap()
    out_dram = nc.dram_tensor("out", [T, D], F32, kind="ExternalOutput").ap()

    def dscr(name, shape, dt):
        kind = "ExternalOutput" if name in taps else "Internal"
        return nc.dram_tensor(name, shape, dt, kind=kind).ap()

    s_wgn = dscr("s_wgn", [DI, D], BF16)
    s_wun = dscr("s_wun", [DI, D], BF16)
    s_wdn = dscr("s_wdn", [D, DI], BF16)
    s_mod = dscr("s_mod", [B, 6 * D], F32)
    s_norm = [dscr("s_norm%d" % b, [N, D], BF16) for b in range(B)]
    s_xq = [dscr("s_xq%d" % b, [N, D], BF16) for b in range(B)]
    s_hq = dscr("s_hq", [N, DI], BF16)
    s_x1 = dscr("s_x1", [N, D], F32)
    s_flat = dscr("s_flat", [2, N], F32)
    s_cnt = dscr("s_cnt", [B, N], F32)
    s_dbg1 = dscr("s_dbg1", [128, D], F32)
    s_dbg2 = dscr("s_dbg2", [128, D], F32)
    s_dbg3 = dscr("s_dbg3", [128, D], F32)
    s_sv = dscr("s_sv", [B, N], F32)

    g = {}

    def sb(name, shape, dt):
        g[name] = es.enter_context(nc.sbuf_tensor("t_" + name, shape, dt))
        return g[name]

    sb("cst", [128, 8], F32)
    sb("ici", [128, 2], I32)
    sb("eye", [128, 128], F32)
    sb("ct", [128, 2 * KD], F32)
    sb("w1pp", [128, 2 * KD], F32)
    sb("ones", [1, 128], F32)
    for nm in ("sa", "ca", "ga", "cb", "sbv", "gb"):
        sb("r_" + nm, [128, D], F32)
    for i in range(2):
        sb("xr%d" % i, [128, D], F32)
    sb("x1r0", [128, D], F32)
    sb("x1r1", [128, D], F32)
    sb("p1n", [128, D], BF16)
    sb("deadb", [128, D], BF16)
    sb("st1", [128, 40], F32)
    sb("st2", [128, 32], F32)
    sb("svec", [128, M], F32)
    sb("cnt4", [128, 4], F32)
    sb("cntr", [128, M], F32)
    sb("srow", [16, 128], F32)
    sb("srow2", [16, 128], F32)
    sb("sBh", [128, D], BF16)
    sb("cBh", [128, D], F32)
    sb("cctz", [128, D], F32)
    sb("ccta", [128, D], F32)
    sb("mrow", [1, 512], F32)
    sb("q_src", [128, 2 * D], F32)
    sb("q_xn", [128, 2 * D], F32)
    sb("q_t", [128, D], I32)
    sb("q_mask", [128, D], U8)
    sb("q_m1", [128, 2 * D], F32)
    sb("q_qd", [128, 2 * D], BF16)
    sb("q_q", [128, 2 * D], BF16)
    sb("q_bm", [128, 128], F32)
    sb("q_sc", [128, 128], F32)
    sb("q_ri", [128, 128], F32)
    sb("silu_g", [128, D], F32)
    sb("hm", [128, D], F32)

    pool = es.enter_context(nc.sbuf_tensor("pool", [128, 96 * 1024], U8))

    def pview(off_kb, cols, dt):
        nb = mybir.dt.size(dt)
        return pool.ap()[:, off_kb * 1024: off_kb * 1024 + cols * nb].bitcast(dt)

    A1 = pview(0, KD * N, BF16)
    CCT = pview(32, KD * N, BF16)
    W2B = pview(64, 16 * D, BF16)
    XQT = pview(0, KD * N, BF16)
    GUP = [pview(32, 8 * D, BF16), pview(48, 8 * D, BF16)]
    UUP = [pview(64, 8 * D, BF16), pview(80, 8 * D, BF16)]
    WDT = pview(0, KI * D, BF16)
    HQT = pview(48, KI * D, BF16)
    WMS = [pview(0, 6 * D, F32), pview(24, 6 * D, F32)]
    MODEV = pview(60, 512, F32)

    R = Rec(nc, es)
    Tk = {}

    def tk(key):
        return Tk.get(key)

    def emit_repl_row(dram_row_ap, dst_ap, dst_w, ps_rep, evac_eng,
                      extra_src_wait=None, evac_kind="copy", sb_src=None):
        last = None
        nchunks = dst_w // 512
        for h in range(nchunks):
            hp = h % 2
            if sb_src is not None:
                tn, r0 = sb_src
                lt = R.o("sp", lambda q, h=h, tn=tn, r0=r0:
                         q.dma_start(
                             out=g["mrow"][0:1, 0:512],
                             in_=g[tn].ap()[r0 + 4 * h:r0 + 4 * h + 4, :]),
                         (extra_src_wait, tk("mrowf")), 16, ctr="mrowl")
            else:
                lt = R.o("sp", lambda q, h=h, dram_row_ap=dram_row_ap:
                         q.dma_start(
                             out=g["mrow"][0:1, 0:512],
                             in_=dram_row_ap[0:1, h * 512:(h + 1) * 512]),
                         (extra_src_wait, tk("mrowf")), 16, ctr="mrowl")
            mm = R.o("pe", lambda q, hp=hp: q.matmul(
                ps_rep[hp][:], g["ones"][:, 0:128],
                g["mrow"][0:1, 0:512], start=True,
                stop=True), (lt, Tk["ones"], tk("psr%d" % hp)))
            Tk["mrowf"] = mm
            if evac_kind == "plus1":
                ev = R.o(evac_eng, lambda q, h=h, hp=hp, dst_ap=dst_ap:
                         q.tensor_scalar(
                             dst_ap[:, h * 512:(h + 1) * 512], ps_rep[hp][:],
                             1.0, None, ALU.add), (mm,))
            elif evac_eng == "ac":
                ev = R.o("ac", lambda q, h=h, hp=hp, dst_ap=dst_ap:
                         q.activation(dst_ap[:, h * 512:(h + 1) * 512],
                                      ps_rep[hp][:], AF.Copy), (mm,))
            else:
                ev = R.o(evac_eng, lambda q, h=h, hp=hp, dst_ap=dst_ap:
                         q.tensor_copy(dst_ap[:, h * 512:(h + 1) * 512],
                                       ps_rep[hp][:]), (mm,))
            Tk["psr%d" % hp] = ev
            last = ev
            if sb_src is not None:
                Tk[("srowf", sb_src[0])] = lt
        return last

    def emit_gate_fix(dst, pre):
        w0 = (pre, tk("q_free_sel"), tk("q_free_st"),
              tk(("xnfree", 0)), tk(("m1free", 0)))
        gxn = g["q_xn"][:, 0:D]
        gm1 = g["q_m1"][:, 0:D]
        t1 = R.o("dv", lambda q, dst=dst: q.tensor_scalar(
            g["q_t"][:], dst.ap().bitcast(I32), g["ici"][:, 0:1], None,
            ALU.bitwise_and), w0)
        t1 = R.o("dv", lambda q: q.tensor_scalar(
            gm1, g["q_t"].ap().bitcast(F32), 0.25, None, ALU.max), ())
        t2 = R.o("dv", lambda q: q.tensor_scalar(
            gxn, gm1, -1.0, None, ALU.mult), ())
        t3 = R.o("dv", lambda q, dst=dst: q.tensor_scalar(
            g["q_mask"][:], dst[:], 0.0, None, ALU.is_lt), ())
        last = R.o("dv", lambda q, dst=dst: q.select(
            dst[:], g["q_mask"][:], gxn, gm1), ())
        Tk["q_free_gate"] = last
        Tk[("xnfree", 0)] = last
        Tk[("m1free", 0)] = last
        return last

    def emit_repl_mod(b, v, dst_name, kind, ps_rep):
        dst = g[dst_name]
        last = emit_repl_row(s_mod[b:b + 1, v * D:(v + 1) * D], dst.ap(), D,
                             ps_rep, "dv", extra_src_wait=Tk["mod_done"],
                             evac_kind=("plus1" if kind == "plus1"
                                        else "copy"))
        if kind == "gate":
            last = emit_gate_fix(dst, last)
            if b == 1 and dst_name == "r_ga" and "s_dbg2" in taps:
                R.o("ac", lambda q, dst=dst: q.dma_start(
                    out=s_dbg2, in_=dst[:]), (last,), 16, ctr="dbg")
            if b == 1 and dst_name == "r_gb" and "s_dbg3" in taps:
                R.o("ac", lambda q, dst=dst: q.dma_start(
                    out=s_dbg3, in_=dst[:]), (last,), 16, ctr="dbg")
        Tk[dst_name] = last

    qctr = [0]

    def emit_quant(src_ap, W, store_emit, pre_waits=(), store_waits=()):
        i = qctr[0] = qctr[0] + 1
        nb = W // 16
        par = (i % 2) * 64
        pw = (i % 2) * D
        bm = g["q_bm"][:, par:par + nb]
        sc = g["q_sc"][:, par:par + nb]
        ri = g["q_ri"][:, par:par + nb]
        xn = g["q_xn"][:, pw:pw + W]
        tq = g["q_t"][:, 0:W]
        msk = g["q_mask"][:, 0:W]
        m1 = g["q_m1"][:, pw:pw + W]
        qq = g["q_q"][:, pw:pw + W]
        qd = g["q_qd"][:, pw:pw + W]
        p2 = i % 2

        w0 = tuple(pre_waits) + (tk("q_free_gate"), Tk["consts"])
        t = R.o("dv", lambda q: q.reduce_max(
            bm, src_ap.rearrange("p (b s) -> p b s", s=16), axis=AX,
            apply_absolute_value=True), w0)
        t = R.o("dv", lambda q: q.tensor_scalar(
            sc, bm, g["cst"][:, 0:1], g["cst"][:, 1:2], ALU.mult, ALU.max),
            (F(t),))
        tri = R.o("dv", lambda q: q.reciprocal(ri, sc), (F(t),))
        txn = R.o("pl", lambda q: q.tensor_tensor(
            xn.rearrange("p (b s) -> p b s", s=16),
            src_ap.rearrange("p (b s) -> p b s", s=16),
            ri.unsqueeze(2).broadcast_to([128, nb, 16]), ALU.mult),
            (F(tri), tk(("xnfree", p2))) + tuple(pre_waits))
        Tk["q_xn_done"] = txn
        tam = R.o("dv", lambda q: q.tensor_scalar(
            tq, xn.bitcast(I32), g["ici"][:, 0:1], None, ALU.bitwise_and),
            (txn,))
        tmask = R.o("dv", lambda q: q.tensor_scalar(
            msk, tq, 0x3F600000, None, ALU.is_gt), ())
        t = R.o("dv", lambda q: q.tensor_scalar(
            tq, xn.bitcast(I32), 21, None, ALU.logical_shift_right), ())
        t = R.o("dv", lambda q: q.tensor_scalar(tq, tq, 1, None, ALU.add),
                ())
        thi = R.o("dv", lambda q: q.tensor_scalar(
            tq, tq, 1, 22, ALU.logical_shift_right, ALU.logical_shift_left),
            ())
        tm = R.o("ac", lambda q: q.activation(m1, xn, AF.Copy, bias=CM,
                                              scale=2.0),
                 (txn, tk(("m1free", p2))))
        tm = R.o("ac", lambda q: q.activation(m1, m1, AF.Copy, bias=-CM / 2,
                                              scale=0.5), (F(tm),))
        Tk[("xnfree", p2)] = tm
        tsel = R.o("dv", lambda q: q.select(qq, msk, tq.bitcast(F32), m1),
                   (tm, tk(("qfree", p2))))
        Tk["q_free_sel"] = tsel
        Tk[("m1free", p2)] = tsel
        tqd = R.o("pl", lambda q: q.tensor_tensor(
            qd.rearrange("p (b s) -> p b s", s=16),
            qq.rearrange("p (b s) -> p b s", s=16),
            sc.unsqueeze(2).broadcast_to([128, nb, 16]), ALU.mult),
            (tsel, tk(("qdfree", p2))))
        Tk[("qfree", p2)] = tqd
        Tk["q_qd_done"] = tqd
        tst = R.o("ac", lambda q: store_emit(q, qd),
                  (tqd,) + tuple(store_waits), 16, ctr="qst")
        Tk[("qdfree", p2)] = tst
        Tk["q_free_st"] = tst
        return tst

    def ffn_quant_jobs():
        jobs = []
        for src, dst, nm in ((wg_in, s_wgn, "wg"), (wu_in, s_wun, "wu")):
            for c in range(KI):
                jobs.append((src, dst, nm, c, 0, D))
        for c in range(KD):
            for (off, W) in PAIRS:
                jobs.append((wd_in, s_wdn, "wd", c, off, W))
        for (j, (src, dst, nm, c, off, W)) in enumerate(jobs):
            par = j % 2
            qs = g["q_src"][:, par * D:par * D + W]
            lt = R.o("sp", lambda q, src=src, c=c, off=off, W=W, qs=qs:
                     q.dma_start(out=qs,
                                 in_=src[c * 128:(c + 1) * 128,
                                         off:off + W]),
                     (tk(("q_src_free", par)),), 16, ctr="wql")

            def store_emit(q, qd, dst=dst, c=c, off=off, W=W):
                return q.dma_start(
                    out=dst[c * 128:(c + 1) * 128, off:off + W], in_=qd)
            tst = emit_quant(qs, W, store_emit, pre_waits=(lt,))
            Tk[("q_src_free", par)] = Tk["q_xn_done"]
            Tk["wq_%s_%d" % (nm, c)] = tst
            yield

    ffnq = ffn_quant_jobs()
    ffnq_alive = [True]

    def ffnq_step(k=1):
        if not ffnq_alive[0]:
            return
        for _ in range(k):
            try:
                next(ffnq)
            except StopIteration:
                ffnq_alive[0] = False
                return

    def p1_chunk(b, m):
        slot = m % 2
        xr = g["xr%d" % slot]
        par = m % 2
        st = g["st1"]
        base = par * 20
        bno = st[:, base:base + 12]
        mean = st[:, base + 12:base + 13]
        var = st[:, base + 13:base + 14]
        rstd = st[:, base + 14:base + 15]
        nmr = st[:, base + 15:base + 16]
        sq2 = st[:, base + 16:base + 17]
        rn = st[:, base + 17:base + 18]
        row0 = b * N + m * 128

        if (b, m) in ((0, 0), (0, 1)):
            tl = Tk[("p1xl", m)]
        else:
            tl = R.o("sp", lambda q, xr=xr, row0=row0: q.dma_start(
                out=xr[:], in_=x_in[row0:row0 + 128, :]),
                (tk(("xrf", slot)),), 16, ctr="xr%d" % slot)
        pf = tk(("p1fin", par))
        t = R.o("dv", lambda q, xr=xr, bno=bno: q.bn_stats(
            bno[:, 0:6], xr[:, 0:512]), (tl, pf))
        tb = R.o("dv", lambda q, xr=xr, bno=bno: q.bn_stats(
            bno[:, 6:12], xr[:, 512:1024]), ())
        tg = R.o("dv", lambda q, bno=bno: q.bn_aggr(
            st[:, base + 12:base + 14], bno), (F(tb),))
        t = R.o("ac", lambda q, var=var, rstd=rstd: q.activation(
            rstd, var, AF.Ln, bias=g["cst"][:, 2:3]), (F(tg),))
        t = R.o("ac", lambda q, rstd=rstd: q.activation(
            rstd, rstd, AF.Exp, scale=-0.5), (F(t),))
        tnm = R.o("dv", lambda q, mean=mean, rstd=rstd, nmr=nmr:
                  q.tensor_scalar(nmr, mean, rstd, -1.0, ALU.mult,
                                  ALU.mult), (F(t),))
        tt1 = R.o("ac", lambda q, xr=xr, nmr=nmr, rstd=rstd: q.activation(
            xr[:], xr[:], AF.Identity, bias=nmr, scale=rstd),
            (F(tnm),))
        th = R.o("dv", lambda q, xr=xr: q.tensor_mul(
            xr[:], xr[:], g["r_ca"][:]), (F(tt1), Tk["r_ca"]))
        th = R.o("pl", lambda q, xr=xr: q.tensor_add(
            xr[:], xr[:], g["r_sa"][:]), (th, Tk["r_sa"]))
        ta2 = R.o("ac", lambda q, xr=xr, sq2=sq2: q.activation(
            g["deadb"][:], xr[:], AF.Square, accum_out=sq2), (th,))
        t = R.o("ac", lambda q, sq2=sq2, rn=rn: q.activation(
            rn, sq2, AF.Ln, bias=g["cst"][:, 3:4]), (F(ta2),))
        t = R.o("ac", lambda q, rn=rn, m=m: q.activation(
            g["svec"][:, m:m + 1], rn, AF.Exp, scale=0.5),
            (F(t), tk("svec_free")))
        t = R.o("ac", lambda q, rn=rn: q.activation(
            rn, rn, AF.Exp, scale=-0.5), (F(t),))
        tn = R.o("dv", lambda q, xr=xr, rn=rn: q.tensor_scalar(
            g["p1n"][:], xr[:], rn, None, ALU.mult),
            (F(t), tk("p1n_free")))
        Tk[("xrf", slot)] = tn
        ts = R.o("ac", lambda q, b=b, m=m: q.dma_start(
            out=s_norm[b][m * 128:(m + 1) * 128, :], in_=g["p1n"][:]),
            (tn,), 16, ctr="nrmst")
        Tk["p1n_free"] = ts
        Tk[("p1fin", par)] = tn
        if m == M - 1:
            Tk[("p1done", b)] = ts

    def emit_col_to_dram(src_col_ap, dram_row, ps_t16, srctok, tn):
        t = R.o("pe", lambda q, src_col_ap=src_col_ap: q.transpose(
            ps_t16[0:M, :], src_col_ap, g["eye"][:]),
            (srctok, Tk["consts"], tk("pst16")))
        t = R.o("dv", lambda q, tn=tn: q.tensor_copy(
            g[tn][0:M, :], ps_t16[0:M, :]),
            (t, tk(("srowf", tn)), tk(("srowfl", tn))))
        Tk["pst16"] = t
        tsv = t
        t = R.o("pl", lambda q, dram_row=dram_row, tn=tn: q.dma_start(
            out=dram_row.rearrange("a (p f) -> (a p) f", p=M),
            in_=g[tn].ap()[0:M, :]),
            (t,), 16, ctr="fl")
        Tk[("srowfl", tn)] = t
        return t, tsv

    t = R.o("sp", lambda q: q.dma_start(out=g["cst"][:], in_=cst_in), (), 16,
            ctr="cstl")
    t = R.o("sp", lambda q: q.dma_start(out=g["ici"][:], in_=ici_in), (), 16,
            ctr="cstl")
    t = R.o("sp", lambda q: q.dma_start(out=g["eye"][:], in_=eye_in), (), 16,
            ctr="cstl")
    t = R.o("sp", lambda q: q.dma_start(
        out=g["ct"].ap().rearrange("p (k c) -> p k c", c=B),
        in_=ct_in.rearrange("(k p) c -> p k c", p=128)), (), 16, ctr="cstl")
    t = R.o("sp", lambda q: q.dma_start(
        out=g["w1pp"].ap().rearrange("p (k c) -> p k c", c=2),
        in_=w1b1_in.rearrange("(k p) c -> p k c", p=128)), (), 16, ctr="cstl")
    Tk["consts"] = t
    Tk["ones"] = R.o("dv", lambda q: q.memset(g["ones"][:], 1.0), ())
    Tk["ctsilu"] = R.o("ac", lambda q: q.activation(
        g["ct"][:], g["ct"][:], AF.Silu), (Tk["consts"],))

    with nc.psum_tensor("psma0", [128, 512], F32) as pa0, \
         nc.psum_tensor("psma1", [128, 512], F32) as pa1, \
         nc.psum_tensor("psma2", [128, 512], F32) as pa2, \
         nc.psum_tensor("psma3", [128, 512], F32) as pa3, \
         nc.psum_tensor("psr0a", [128, 512], F32) as pr0, \
         nc.psum_tensor("psr1a", [128, 512], F32) as pr1:
        pma = [pa0, pa1, pa2, pa3]
        ps_rep = [pr0, pr1]
        slab_tok = [None, None]
        for pm in range(2):
            Tk[("p1xl", pm)] = R.o(
                "sp", lambda q, pm=pm: q.dma_start(
                    out=g["xr%d" % pm][:],
                    in_=x_in[pm * 128:(pm + 1) * 128, :]), (), 16,
                ctr="xr%d" % pm)
        mm = None
        for k in range(KD):
            sl = k % 2
            lt = R.o("sp", lambda q, k=k, sl=sl: q.dma_start(
                out=WMS[sl][:], in_=wmod_in[k * 128:(k + 1) * 128, :]),
                (slab_tok[sl],), 16, ctr="wm%d" % sl)
            for n in range(12):
                po = 32 * (n % 3)
                mm = R.o("pe", lambda q, k=k, n=n, sl=sl, po=po: q.matmul(
                    pma[n // 3][po:po + 2, :],
                    g["ct"][:, 2 * k:2 * k + 2],
                    WMS[sl][:, n * 512:(n + 1) * 512],
                    start=(k == 0), stop=(k == KD - 1)),
                    (lt, Tk["ctsilu"]))
            slab_tok[sl] = mm
            ffnq_step(1)
        Tk["mod_mm_done"] = mm
        tmst = None
        for bk in range(4):
            tev = R.o("dv", lambda q, bk=bk: q.tensor_copy(
                MODEV[0:66, :], pma[bk][0:66, :]), (mm, tmst))
            for c in range(B):
                tmst = R.o(
                    "ac", lambda q, bk=bk, c=c: q.dma_start(
                        out=s_mod[c:c + 1, 3 * bk * 512:(3 * bk + 3) * 512]
                        .rearrange("a (n w) -> (a n) w", w=512),
                        in_=MODEV[c:c + 65:32, :]),
                    (tev,), 16, ctr="modst")
        Tk["mod_done"] = tmst

        emit_repl_mod(0, 0, "r_sa", "plain", ps_rep)
        emit_repl_mod(0, 1, "r_ca", "plus1", ps_rep)
        emit_repl_mod(0, 2, "r_ga", "gate", ps_rep)
        emit_repl_mod(0, 3, "r_sbv", "plain", ps_rep)
        emit_repl_mod(0, 4, "r_cb", "plus1", ps_rep)
        emit_repl_mod(0, 5, "r_gb", "gate", ps_rep)

        Tk["w2pre"] = R.o("pl", lambda q: q.dma_start(
            out=W2B.rearrange("p (k d) -> p k d", d=D),
            in_=w2_in.rearrange("(k p) d -> p k d", p=128)),
            (Tk["mod_mm_done"],), 16, ctr="w2l")
        for m in range(M):
            p1_chunk(0, m)
            ffnq_step(2)
        R.flush()

    for b in range(B):
        with nc.psum_tensor("ps_sim%d" % b, [128, 4 * 512], F32) as ps_sim, \
             nc.psum_tensor("pst16%d" % b, [16, 128], F32) as ps_t16, \
             nc.psum_tensor("psr0b%d" % b, [128, 512], F32) as pr0, \
             nc.psum_tensor("psr1b%d" % b, [128, 512], F32) as pr1:
            ps_rep = [pr0, pr1]
            for k in range(KD):
                Tk[("A1", k)] = R.o("sp", lambda q, k=k, b=b:
                          q.dma_start_transpose(
                    A1[:, k * N:(k + 1) * N],
                    s_norm[b][:, k * 128:(k + 1) * 128]),
                    (tk(("p1done", b)), tk("pool_lo_free"),
                     tk("pool_hi_free"), Tk["mod_mm_done"]), 16, ctr="a1l")

            if b == 1:
                emit_repl_mod(1, 2, "r_ga", "gate", ps_rep)
                emit_repl_mod(1, 3, "r_sbv", "plain", ps_rep)
                emit_repl_mod(1, 4, "r_cb", "plus1", ps_rep)
                emit_repl_mod(1, 5, "r_gb", "gate", ps_rep)

            tfl, tsv = emit_col_to_dram(g["svec"][:, 0:M], s_flat[0:1, :],
                                        ps_t16, tk(("p1done", b)),
                                        "srow2")
            Tk["svec_free"] = tsv
            if "s_sv" in taps:
                R.o("pl", lambda q, b=b: q.dma_start(
                    out=s_sv[b:b + 1, :], in_=s_flat[0:1, :]), (tfl,), 16, ctr="fl")

            mmlast = None
            for n in range(4):
                Tk[("simev", -1, n)] = tk(("simev", M - 1, n))
            for m in range(M):
                for k in range(KD):
                    for n in range(4):
                        w = [Tk[("A1", k)]]
                        if k == 0:
                            w.append(tk(("simev", m - 1, n)))
                        mmlast = R.o("pe", lambda q, k=k, n=n, m=m: q.matmul(
                            ps_sim[:, n * 512:(n + 1) * 512],
                            A1[:, k * N + m * 128:k * N + m * 128 + 128],
                            A1[:, k * N + n * 512:k * N + (n + 1) * 512],
                            start=(k == 0), stop=(k == KD - 1)), w)
                Tk[("simmm", m)] = mmlast
                for n in (0,):
                    Tk[("simev", m, n)] = R.o(
                        "dv", lambda q, n=n: q.tensor_scalar(
                            g["deadb"][:, 0:512],
                            ps_sim[:, n * 512:(n + 1) * 512], 0.0, None,
                            ALU.max, ALU.add,
                            accum_out=g["cnt4"][:, n:n + 1]),
                        (Tk[("simmm", m)],))
                for n in (1, 2, 3):
                    Tk[("simev", m, n)] = R.o(
                        "ac", lambda q, n=n: q.activation(
                            g["deadb"][:, 512:1024],
                            ps_sim[:, n * 512:(n + 1) * 512], AF.Relu,
                            accum_out=g["cnt4"][:, n:n + 1]),
                        (Tk[("simmm", m)], tk(("cnt", m - 1))))
                Tk[("cnt", m)] = R.o("dv", lambda q, m=m: q.reduce_sum(
                    g["cntr"][:, m:m + 1], g["cnt4"][:, 0:4], axis=AX),
                    (F(Tk[("simev", m, 0)]),
                     Tk[("simev", m, 3)]))
                if b == 0:
                    ffnq_step(2)

            tfc, tcv = emit_col_to_dram(g["cntr"][:, 0:M], s_flat[1:2, :],
                                        ps_t16, F(Tk[("cnt", M - 1)]),
                                        "srow")
            if "s_cnt" in taps:
                R.o("pl", lambda q, b=b: q.dma_start(
                    out=s_cnt[b:b + 1, :], in_=s_flat[1:2, :]), (tfc,), 16, ctr="fl")

            for half in range(2):
                hs = half * 1024
                tcb = emit_repl_row(None, g["cBh"].ap(),
                                    1024, ps_rep, "dv", extra_src_wait=tcv,
                                    sb_src=("srow", 8 * half))
                tsb = emit_repl_row(None, g["sBh"].ap(),
                                    1024, ps_rep, "ac", extra_src_wait=tsv,
                                    sb_src=("srow2", 8 * half))
                for k in range(KD):
                    tz = R.o("dv", lambda q, k=k: q.tensor_scalar(
                        g["cctz"][:], g["cBh"][:],
                        g["w1pp"][:, 2 * k:2 * k + 1],
                        g["w1pp"][:, 2 * k + 1:2 * k + 2],
                        ALU.mult, ALU.add), (tcb, tk("cct_free")))
                    t = R.o("dv", lambda q: q.tensor_scalar(
                        g["ccta"].ap().bitcast(I32),
                        g["cctz"].ap().bitcast(I32),
                        g["ici"][:, 0:1], None, ALU.bitwise_and),
                        (F(tz), tk("ccta_free")))
                    t = R.o("ac", lambda q: q.activation(
                        g["ccta"][:], g["ccta"][:], AF.Exp, scale=-1.0),
                        (t,))
                    t = R.o("ac", lambda q: q.activation(
                        g["ccta"][:], g["ccta"][:], AF.Ln, bias=1.0),
                        (F(t),))
                    t = R.o("dv", lambda q, k=k, hs=hs:
                            q.scalar_tensor_tensor(
                        CCT[:, k * N + hs:k * N + hs + 1024], g["cctz"][:],
                        0.0, g["ccta"][:], ALU.max, ALU.add),
                        (t, tk("pool_lo_free"), Tk[("simmm", M - 1)]))
                    Tk["cct_free"] = t
                    Tk["ccta_free"] = t
                    Tk[("CCT", k, half)] = t
                for k in range(KD):
                    Tk[("hT", k, half)] = R.o(
                        "dv", lambda q, k=k, hs=hs: q.tensor_mul(
                            A1[:, k * N + hs:k * N + hs + 1024],
                            A1[:, k * N + hs:k * N + hs + 1024],
                            g["sBh"][:]), (tsb, Tk[("simmm", M - 1)]))
            R.flush()

        with nc.psum_tensor("ps_at0%d" % b, [128, D], F32) as pa0, \
             nc.psum_tensor("ps_at1%d" % b, [128, D], F32) as pa1, \
             nc.psum_tensor("ps_at2%d" % b, [128, D], F32) as pa2, \
             nc.psum_tensor("psr0c%d" % b, [128, 512], F32) as pr0, \
             nc.psum_tensor("psr1c%d" % b, [128, 512], F32) as pr1:
            ps_at = [pa0, pa1, pa2]
            ps_rep = [pr0, pr1]
            tw2 = Tk["w2pre"]

            if b == 0:
                emit_repl_mod(1, 0, "r_sa", "plain", ps_rep)
                emit_repl_mod(1, 1, "r_ca", "plus1", ps_rep)

            mmlast = None
            Tk[("hmfw2", 0)] = tk("hm_free")
            Tk[("hmfw2", 1)] = tk(("x1rf", 0))
            for dd in range(1, 4):
                Tk[("atev", -dd)] = tk(("atev", M - dd))
            for m in range(M):
                slot = m % 2
                xr = g["xr%d" % slot]
                row0 = b * N + m * 128
                tl = R.o("sp", lambda q, xr=xr, row0=row0: q.dma_start(
                    out=xr[:], in_=x_in[row0:row0 + 128, :]),
                    (tk(("xrf", slot)), tk(("x1stp", slot))), 16,
                    ctr="xr%d" % slot)
                half = m // 8
                for k in range(16):
                    kk = k % 8
                    lhs = A1 if k < 8 else CCT
                    w = [Tk[("hT", kk, half)] if k < 8
                         else Tk[("CCT", kk, half)], tw2]
                    if k == 0:
                        w.append(tk(("atev", m - 3)))
                    for n in range(2):
                        mmlast = R.o("pe", lambda q, k=k, kk=kk, n=n, m=m,
                                     lhs=lhs: q.matmul(
                            ps_at[m % 3][:, n * 512:(n + 1) * 512],
                            lhs[:, kk * N + m * 128:kk * N + m * 128 + 128],
                            W2B[:, k * D + n * 512:k * D + (n + 1) * 512],
                            start=(k == 0), stop=(k == 15)), w)
                Tk[("atmm", m)] = mmlast
                x1t = xr
                t1 = R.o("dv", lambda q, m=m: q.tensor_mul(
                    g["silu_g"][:], ps_at[m % 3][:], g["r_ga"][:]),
                    (Tk[("atmm", m)], Tk["r_ga"], tk("t1_free")))
                Tk[("atev", m)] = t1
                tx1 = R.o("pl", lambda q, x1t=x1t: q.tensor_add(
                    x1t[:], g["silu_g"][:], x1t[:]), (t1, tl))
                Tk["t1_free"] = tx1
                tsx = R.o("ac", lambda q, x1t=x1t, m=m: q.dma_start(
                    out=s_x1[m * 128:(m + 1) * 128, :], in_=x1t[:]),
                    (tx1, tk("s_x1_free")), 16, ctr="x1st%d" % (m % 2))
                Tk[("s_x1", m)] = tsx
                Tk[("x1stp", m % 2)] = tsx

                par = m % 2
                st = g["st2"]
                base = par * 16
                bno = st[:, base:base + 12]
                mean = st[:, base + 12:base + 13]
                var = st[:, base + 13:base + 14]
                rstd = st[:, base + 14:base + 15]
                nmr = st[:, base + 15:base + 16]
                lf = tk(("ln2fin", par))
                t = R.o("dv", lambda q, x1t=x1t, bno=bno: q.bn_stats(
                    bno[:, 0:6], x1t[:, 0:512]), (tx1, lf))
                tb = R.o("dv", lambda q, x1t=x1t, bno=bno: q.bn_stats(
                    bno[:, 6:12], x1t[:, 512:1024]), ())
                tg = R.o("dv", lambda q, base=base: q.bn_aggr(
                    st[:, base + 12:base + 14], st[:, base:base + 12]),
                    (F(tb),))
                t = R.o("ac", lambda q, var=var, rstd=rstd: q.activation(
                    rstd, var, AF.Ln, bias=g["cst"][:, 2:3]), (F(tg),))
                t = R.o("ac", lambda q, rstd=rstd: q.activation(
                    rstd, rstd, AF.Exp, scale=-0.5), (F(t),))
                t = R.o("dv", lambda q, mean=mean, rstd=rstd, nmr=nmr:
                        q.tensor_scalar(nmr, mean, rstd, -1.0, ALU.mult,
                                        ALU.mult), (F(t),))
                hmw = g["hm"] if par == 0 else g["x1r0"]
                th2 = R.o("ac", lambda q, x1t=x1t, nmr=nmr, rstd=rstd,
                          hmw=hmw: q.activation(hmw[:], x1t[:], AF.Identity,
                                                bias=nmr, scale=rstd),
                          (F(t), tk(("hmfw2", par))))
                Tk[("ln2fin", par)] = th2
                Tk[("xrf", slot)] = th2
                th2 = R.o("dv", lambda q, hmw=hmw: q.tensor_mul(
                    hmw[:], hmw[:], g["r_cb"][:]),
                    (th2, Tk["r_cb"]))
                th2 = R.o("pl", lambda q, hmw=hmw: q.tensor_add(
                    hmw[:], hmw[:], g["r_sbv"][:]),
                    (th2, Tk["r_sbv"]))

                def store_xq(q, qd, b=b, m=m):
                    return q.dma_start(
                        out=s_xq[b][m * 128:(m + 1) * 128, :], in_=qd)
                tst = emit_quant(hmw[:], D, store_xq, pre_waits=(th2,))
                Tk[("hmfw2", par)] = Tk["q_xn_done"]
                ffnq_step(1)
            Tk["A1_lastuse"] = Tk[("atmm", M - 1)]
            Tk["xq_done"] = Tk["q_free_st"]
            Tk["hm_free"] = Tk[("hmfw2", 0)]
            R.flush()

        with nc.psum_tensor("ps_g0%d" % b, [128, D], F32) as pg0, \
             nc.psum_tensor("ps_g1%d" % b, [128, D], F32) as pg1, \
             nc.psum_tensor("ps_u0%d" % b, [128, D], F32) as pu0, \
             nc.psum_tensor("ps_u1%d" % b, [128, D], F32) as pu1:
            ps_g = [pg0, pg1]
            ps_u = [pu0, pu1]
            for k in range(KD):
                Tk[("XQT", k)] = R.o("sp", lambda q, k=k, b=b:
                          q.dma_start_transpose(
                    XQT[:, k * N:(k + 1) * N],
                    s_xq[b][:, k * 128:(k + 1) * 128]),
                    (Tk["xq_done"], Tk["A1_lastuse"]), 16, ctr="xqtl")

            p1_left = list(range(M)) if b == 0 else []
            it = 0
            Tk[("hmfree", 0)] = tk("hm_free")
            Tk[("guev", -2)] = tk(("guev", 3 * M - 2))
            Tk[("guev", -1)] = tk(("guev", 3 * M - 1))
            for (p, (poff, Wp)) in enumerate(PAIRS):
                pp = p % 2
                wq_wait = Tk["wq_wu_%d" % (KI - 1)]
                for k in range(KD):
                    tgl = R.o(
                        "sp", lambda q, k=k, pp=pp, poff=poff, Wp=Wp:
                        q.dma_start_transpose(
                            GUP[pp][:, k * D:k * D + Wp],
                            s_wgn[poff:poff + Wp, k * 128:(k + 1) * 128]),
                        (wq_wait, tk(("gup_free", pp)), Tk["A1_lastuse"],
                         tk("pool_hi_free")), 16, ctr="gupl%d" % pp)
                    tul = R.o(
                        "sp", lambda q, k=k, pp=pp, poff=poff, Wp=Wp:
                        q.dma_start_transpose(
                            UUP[pp][:, k * D:k * D + Wp],
                            s_wun[poff:poff + Wp, k * 128:(k + 1) * 128]),
                        (tk(("uup_free", pp)), Tk["A1_lastuse"],
                         tk("pool_hi_free")), 16, ctr="uupl%d" % pp)
                    Tk[("gup", pp, k)] = tgl
                    Tk[("uup", pp, k)] = tul
                nh = (Wp + 511) // 512
                for m in range(M):
                    itp = it % 2
                    mmlast = None
                    for k in range(KD):
                        w = [Tk[("XQT", k)], Tk[("gup", pp, k)],
                             Tk[("uup", pp, k)]]
                        if k == 0:
                            w.append(tk(("guev", it - 2)))
                        for n in range(nh):
                            c0 = n * 512
                            c1 = min(Wp, (n + 1) * 512)
                            R.o("pe", lambda q, k=k, m=m, pp=pp, c0=c0,
                                c1=c1, itp=itp: q.matmul(
                                ps_g[itp][:, c0:c1],
                                XQT[:, k * N + m * 128:k * N + m * 128 + 128],
                                GUP[pp][:, k * D + c0:k * D + c1],
                                start=(k == 0), stop=(k == KD - 1)), w)
                            mmlast = R.o("pe", lambda q, k=k, m=m, pp=pp,
                                         c0=c0, c1=c1, itp=itp: q.matmul(
                                ps_u[itp][:, c0:c1],
                                XQT[:, k * N + m * 128:k * N + m * 128 + 128],
                                UUP[pp][:, k * D + c0:k * D + c1],
                                start=(k == 0), stop=(k == KD - 1)), ())
                    Tk[("gumm", it)] = mmlast
                    hmv = (g["hm"] if itp == 0 else g["q_src"])[:, 0:Wp]
                    hw = [None, tk(("hmfree", itp))]
                    if it == 1:
                        hw += [tk(("q_src_free", 0)), tk(("q_src_free", 1))]
                    tsg = R.o("ac", lambda q, Wp=Wp, itp=itp: q.activation(
                        g["silu_g"][:, 0:Wp], ps_g[itp][:, 0:Wp], AF.Silu),
                        (Tk[("gumm", it)], tk("t1_free")))
                    hw[0] = tsg
                    thm = R.o("dv", lambda q, Wp=Wp, itp=itp, hmv=hmv:
                              q.tensor_mul(hmv, g["silu_g"][:, 0:Wp],
                                           ps_u[itp][:, 0:Wp]), tuple(hw))
                    Tk["t1_free"] = thm
                    Tk[("guev", it)] = thm

                    def store_hq(q, qd, m=m, poff=poff, Wp=Wp):
                        return q.dma_start(
                            out=s_hq[m * 128:(m + 1) * 128, poff:poff + Wp],
                            in_=qd)
                    tst = emit_quant(hmv, Wp, store_hq,
                                     pre_waits=(thm,),
                                     store_waits=(tk("s_hq_free"),))
                    Tk[("hmfree", itp)] = Tk["q_xn_done"]
                    Tk[("hq", p, m)] = tst
                    it += 1
                Tk[("gup_free", pp)] = Tk[("gumm", it - 1)]
                Tk[("uup_free", pp)] = Tk[("gumm", it - 1)]
            Tk["hq_done"] = Tk[("hq", 2, M - 1)]
            Tk["gu_lastmm"] = Tk[("gumm", it - 1)]
            Tk["hm_free"] = Tk[("hmfree", 0)]
            R.flush()

        with nc.psum_tensor("ps_d0%d" % b, [128, D], F32) as pd0, \
             nc.psum_tensor("ps_d1%d" % b, [128, D], F32) as pd1, \
             nc.psum_tensor("ps_d2%d" % b, [128, D], F32) as pd2, \
             nc.psum_tensor("ps_d3%d" % b, [128, D], F32) as pd3:
            ps_d = [pd0, pd1, pd2, pd3]
            for ki in range(KI):
                twd = R.o("sp", lambda q, ki=ki:
                          q.dma_start_transpose(
                    WDT[:, ki * D:(ki + 1) * D],
                    s_wdn[:, ki * 128:(ki + 1) * 128]),
                    (Tk["wq_wd_%d" % (KD - 1)], Tk["gu_lastmm"],
                     Tk["xq_done"]), 16, ctr="wdtl")
                Tk[("WDT", ki)] = twd
            gbtok = Tk["r_gb"]
            for dd in range(1, 5):
                Tk[("dnev", -dd)] = tk(("dnev", M - dd))

            mmlast = None
            for half in range(2):
                for ki in range(KI):
                    thq = R.o("sp", lambda q, ki=ki, half=half:
                              q.dma_start_transpose(
                        HQT[:, ki * D:(ki + 1) * D],
                        s_hq[half * 1024:(half + 1) * 1024,
                             ki * 128:(ki + 1) * 128]),
                        (Tk["hq_done"], Tk["gu_lastmm"],
                         tk(("hqt_used", ki))), 16, ctr="hqtl")
                    Tk[("HQT", ki)] = thq
                for mm_i in range(8):
                    m = half * 8 + mm_i
                    x1r = g["x1r0" if m % 2 == 0 else "x1r1"]
                    tl = R.o("sp", lambda q, x1r=x1r, m=m: q.dma_start(
                        out=x1r[:], in_=s_x1[m * 128:(m + 1) * 128, :]),
                        (Tk[("s_x1", m)], tk(("x1rf", m % 2)))
                        + ((tk("xq_done"),) if m == 0 else ()), 16,
                        ctr="x1r%d" % (m % 2))
                    for ki in range(KI):
                        w = [Tk[("WDT", ki)], Tk[("HQT", ki)]]
                        if ki == 0:
                            w.append(tk(("dnev", m - 4)))
                        for n in range(2):
                            mmlast = R.o("pe", lambda q, ki=ki, m=m,
                                         mm_i=mm_i, n=n: q.matmul(
                                ps_d[m % 4][:, n * 512:(n + 1) * 512],
                                HQT[:, ki * D + mm_i * 128:
                                    ki * D + mm_i * 128 + 128],
                                WDT[:, ki * D + n * 512:
                                    ki * D + (n + 1) * 512],
                                start=(ki == 0), stop=(ki == KI - 1)), w)
                        Tk[("hqt_used", ki)] = mmlast
                    Tk[("dnmm", m)] = mmlast
                    t3 = R.o("dv", lambda q, m=m: q.tensor_mul(
                        g["silu_g"][:], ps_d[m % 4][:], g["r_gb"][:]),
                        (Tk[("dnmm", m)], gbtok, tk("t1_free")))
                    Tk[("dnev", m)] = t3
                    to = R.o("pl", lambda q, x1r=x1r:
                             q.tensor_add(x1r[:], g["silu_g"][:], x1r[:]),
                             (t3, tl))
                    Tk["t1_free"] = to
                    tso = R.o("ac", lambda q, x1r=x1r, m=m, b=b:
                              q.dma_start(
                        out=out_dram[b * N + m * 128:
                                     b * N + (m + 1) * 128, :],
                        in_=x1r[:]), (to,), 16, ctr="outst%d" % (m % 2))
                    Tk[("x1rf", m % 2)] = tso
                    Tk[("outst", b, m)] = tso
                    if b == 0:
                        if m < 10:
                            p1_chunk(1, m)
                        elif m < 13:
                            p1_chunk(1, 10 + 2 * (m - 10))
                            p1_chunk(1, 11 + 2 * (m - 10))
            Tk["s_x1_free"] = tk(("x1rf", 1))
            Tk["s_hq_free"] = Tk[("dnmm", M - 1)]
            Tk["pool_lo_free"] = Tk[("dnmm", M - 1)]
            Tk["pool_hi_free"] = Tk[("dnmm", M - 1)]
            if b == 0:
                Tk["w2pre"] = R.o("pl", lambda q: q.dma_start(
                    out=W2B.rearrange("p (k d) -> p k d", d=D),
                    in_=w2_in.rearrange("(k p) d -> p k d", p=128)),
                    (Tk["pool_hi_free"],), 16, ctr="w2l")
            R.flush()

    v0 = R.cnt["outst0"]
    v1 = R.cnt["outst1"]
    R.o("sp", lambda q: q.wait_ge(R.sem["outst0"], v0), (), 0)
    R.o("sp", lambda q: q.wait_ge(R.sem["outst1"], v1), (), 0)
    R.flush()

    es.close()
    return nc


def _in_maps(L0_attn_data, L0_attn_W1, L0_attn_b1, L0_attn_W2, L0_ffn_w_gate,
             L0_ffn_w_up, L0_ffn_w_down, L0_c, L0_wmod):
    x = np.asarray(L0_attn_data, dtype=np.float32)
    w1 = np.asarray(L0_attn_W1, dtype=np.float32)
    b1 = np.asarray(L0_attn_b1, dtype=np.float32)
    w2 = np.ascontiguousarray(np.asarray(L0_attn_W2, dtype=np.float32))
    wg = np.ascontiguousarray(np.asarray(L0_ffn_w_gate, dtype=np.float32))
    wu = np.ascontiguousarray(np.asarray(L0_ffn_w_up, dtype=np.float32))
    wd = np.ascontiguousarray(np.asarray(L0_ffn_w_down, dtype=np.float32))
    c = np.asarray(L0_c, dtype=np.float32)
    wmod = np.ascontiguousarray(np.asarray(L0_wmod, dtype=np.float32))

    w1b1 = np.ascontiguousarray(np.stack([w1.reshape(D), b1.reshape(D)],
                                         axis=1))
    cst = np.zeros((128, 8), np.float32)
    cst[:, 0] = np.float32(1.0 / 6.0)
    cst[:, 1] = np.float32(1e-12)
    cst[:, 2] = np.float32(1e-5)
    cst[:, 3] = np.float32(1e-12)
    ici = np.zeros((128, 2), np.int32)
    ici[:, 0] = np.int32(0x7FFFFFFF)
    eye = np.eye(128, dtype=np.float32)

    maps = []
    for i in range(NCORES):
        maps.append({
            "x": np.ascontiguousarray(x[2 * i:2 * i + 2].reshape(T, D)),
            "c2t": np.ascontiguousarray(c[2 * i:2 * i + 2].T),
            "w1b1": w1b1, "w2": w2, "wg": wg, "wu": wu, "wd": wd,
            "wmod": wmod, "cst": cst, "ici": ici, "eye": eye,
        })
    return maps


def _run(taps=(), core_ids=None, **inputs):
    nc = build(taps=taps)
    maps = _in_maps(**inputs)
    ids = list(range(NCORES)) if core_ids is None else core_ids
    res = run_bass_kernel_spmd(nc, maps[:len(ids)], core_ids=ids)
    return res


def kernel(**inputs):
    res = _run(**inputs)
    out = np.concatenate(
        [np.asarray(r["out"], np.float32).reshape(B, N, D)
         for r in res.results], axis=0)
    return out
